# version 13
# baseline (speedup 1.0000x reference)
import numpy as np

DEBUG_DUMPS = False
PHASE_MARKS = []
PHASE_LIMIT = None
H1_TILES = 32
H1_SKIP = set()

import concourse.bass as bass
import concourse.mybir as mybir
from concourse.bass_utils import run_bass_kernel_spmd

F32 = mybir.dt.float32
BF16 = mybir.dt.bfloat16
U8 = mybir.dt.uint8
AF = mybir.ActivationFunctionType
ALU = mybir.AluOpType
AX = mybir.AxisListType.X

D = 2048
DC = 16
S = 4096
T = 1024
NG = 8
E = 8
INV_SQRT_HD = float(1.0 / np.sqrt(128.0))
LN_EPS = 1e-5
RMS_EPS = 1e-6
KB = 1024
NEG_BIG = -1e30


def _place(nc, backing, name, shape, dtype, off, span=None):
    per_part = int(np.prod(shape[1:])) * mybir.dt.size(dtype)
    limit = span if span is not None else int(np.prod(backing.shape[1:])) * mybir.dt.size(backing.dtype)
    assert off + per_part <= limit, (name, off, per_part, limit)
    base = nc.lookup_mloc(backing).addr
    return nc.alloc_sbuf_tensor_at(name, list(shape), dtype, offset=base + off)


class Sched:

    ENGINES = ("sync", "tensor", "vector", "scalar", "gpsimd")
    OWN = {"vector": "sV", "scalar": "sS", "gpsimd": "sG"}

    def __init__(self):
        self.ops = {e: [] for e in self.ENGINES}
        self.cnt = {}
        self.lastwait = {e: {} for e in self.ENGINES}
        self.frozen = False

    def n(self, sem):
        return self.cnt.get(sem, 0)

    def op(self, eng, fn, inc=None, waits=(), auto=True):
        if self.frozen:
            return self.n(inc[0]) if inc else None
        own = self.OWN.get(eng)
        waits = list(waits)
        if own is not None and auto:
            waits.append((own, self.n(own)))
            if inc is None:
                inc = (own, 1)
        real = []
        for s, v in waits:
            if v is None or v <= 0:
                continue
            if self.lastwait[eng].get(s, 0) >= v:
                continue
            self.lastwait[eng][s] = v
            real.append((s, v))
        self.ops[eng].append((fn, inc, tuple(real)))
        if inc is not None:
            s, k = inc
            self.cnt[s] = self.cnt.get(s, 0) + k
        return self.n(inc[0]) if inc else None


def build_program():
    nc = bass.Bass("TRN2", target_bir_lowering=False)

    xb = nc.dram_tensor("xb", [S, D], F32, kind="ExternalInput")
    wq_t = nc.dram_tensor("wq_t", [D, 2048], F32, kind="ExternalInput")
    wk_t = nc.dram_tensor("wk_t", [D, 1024], F32, kind="ExternalInput")
    wv_t = nc.dram_tensor("wv_t", [D, 1024], F32, kind="ExternalInput")
    wo_t = nc.dram_tensor("wo_t", [D, 2048], F32, kind="ExternalInput")
    router_w = nc.dram_tensor("router_w", [D, E], F32, kind="ExternalInput")
    router_b = nc.dram_tensor("router_b", [1, E], F32, kind="ExternalInput")
    expert_w = nc.dram_tensor("expert_w", [E, D, D], F32, kind="ExternalInput")
    expert_b = nc.dram_tensor("expert_b", [E, D], F32, kind="ExternalInput")
    proj_w = nc.dram_tensor("proj_w", [D, D], F32, kind="ExternalInput")
    proj_b = nc.dram_tensor("proj_b", [1, D], F32, kind="ExternalInput")
    norm_w = nc.dram_tensor("norm_w", [1, D], F32, kind="ExternalInput")
    cvec = nc.dram_tensor("cvec", [1, D], F32, kind="ExternalInput")
    wmod = nc.dram_tensor("wmod", [D, 6 * D], F32, kind="ExternalInput")
    yout = nc.dram_tensor("yout", [T, D], F32, kind="ExternalOutput")
    if DEBUG_DUMPS:
        dbg_h1 = nc.dram_tensor("dbg_h1", [128, DC, S], BF16, kind="ExternalOutput")
        dbg_att = nc.dram_tensor("dbg_att", [128, 8, D], BF16, kind="ExternalOutput")
        dbg_mod = nc.dram_tensor("dbg_mod", [128, 6, DC], F32, kind="ExternalOutput")
        dbg_kg = nc.dram_tensor("dbg_kg", [128, S], BF16, kind="ExternalOutput")
        dbg_qg = nc.dram_tensor("dbg_qg", [128, 2, T], BF16, kind="ExternalOutput")
        dbg_vt = nc.dram_tensor("dbg_vt", [128, 32, 132], BF16, kind="ExternalOutput")
        dbg_ms = nc.dram_tensor("dbg_ms", [1, NG * 16], F32, kind="ExternalOutput")

    arena = nc.alloc_sbuf_tensor("arena", [128, 128 * KB], U8)
    r1 = nc.alloc_sbuf_tensor("r1", [128, 32 * KB], U8)
    r245 = nc.alloc_sbuf_tensor("r245", [128, 32 * KB], U8)
    r3 = nc.alloc_sbuf_tensor("r3", [128, 8 * KB], U8)

    mod_row = _place(nc, r1, "mod_row", [1, 6 * D], BF16, 0)
    h1 = _place(nc, arena, "h1", [128, DC, S], BF16, 0)
    attn_fm = _place(nc, arena, "attn_fm", [128, DC, T], BF16, 0)
    combined = _place(nc, arena, "combined", [128, DC, T], BF16, 0)
    pb_b = _place(nc, arena, "pb_b", [128, D], F32, 16 * KB)
    s3_b = _place(nc, arena, "s3_b", [128, D], F32, 24 * KB)
    h2 = _place(nc, arena, "h2", [128, DC, T], BF16, 32 * KB)
    wo_bf = _place(nc, arena, "wo_bf", [128, DC, 1024], BF16, 32 * KB)
    x2 = _place(nc, arena, "x2", [128, 8, D], F32, 64 * KB)

    kg = _place(nc, r1, "kg", [128, S], BF16, 0)
    vtm = _place(nc, r1, "vtm", [128, 32, 132], BF16, 8 * KB)
    qg = _place(nc, r1, "qg", [128, 2, T], BF16, 17 * KB)
    qkvbf = _place(nc, r1, "qkvbf", [128, 4, 1024], BF16, 21 * KB)
    ga_b = _place(nc, r1, "ga_b", [128, D], F32, 0)
    wbc = _place(nc, r1, "wbc", [128, 2, T], F32, 0)
    tmp_bf = _place(nc, r1, "tmp_bf", [128, 2, 512], BF16, 8 * KB)
    silu_r = _place(nc, r1, "silu_r", [128, 2, 512], BF16, 10 * KB)
    webf = _place(nc, r1, "webf", [128, 4, 16, 128], BF16, 12 * KB)
    befm = _place(nc, r1, "befm", [128, 2, DC], F32, 28 * KB)
    proj_out = _place(nc, r1, "proj_out", [128, 8, D], BF16, 0)

    xst = _place(nc, r245, "xst", [128, 2, D], F32, 0)
    pslab = _place(nc, r245, "pslab", [128, DC, 512], BF16, 0)
    attn_tm = _place(nc, r245, "attn_tm", [128, 8, D], BF16, 0)
    u_ring = _place(nc, r245, "u_ring", [128, 2, D], BF16, 16 * KB)
    h2tm = _place(nc, r245, "h2tm", [128, 2, D], F32, 0)
    y_buf = _place(nc, r245, "y_buf", [128, 2, D], F32, 16 * KB)

    wm_bf = _place(nc, r3, "wm_bf", [128, 2, 2048], BF16, 0)
    wm_bf2 = _place(nc, r245, "wm_bf2", [128, 2, 2048], BF16, 24 * KB)
    pt = _place(nc, r3, "pt", [128, 2, 1024], BF16, 0)
    row0b = _place(nc, r3, "row0b", [1, 2, 512], F32, 4 * KB)
    rowb = _place(nc, r3, "rowb", [1, 512], F32, 6 * KB)

    ident = nc.alloc_sbuf_tensor("ident", [128, 128], BF16)
    identf = nc.alloc_sbuf_tensor("identf", [128, 128], F32)
    ones_row = nc.alloc_sbuf_tensor("ones_row", [1, 128], F32)
    ones_col_bf = nc.alloc_sbuf_tensor("ones_col_bf", [128, 1], BF16)
    modfm = nc.alloc_sbuf_tensor("modfm", [128, 6, DC], F32)
    gsc = nc.alloc_sbuf_tensor("gsc", [128, DC], F32)
    gsc2 = nc.alloc_sbuf_tensor("gsc2", [128, DC], F32)
    bno = nc.alloc_sbuf_tensor("bno", [128, 2, 24], F32)
    bna = nc.alloc_sbuf_tensor("bna", [128, 2, 2], F32)
    st1 = nc.alloc_sbuf_tensor("st1", [128, 2, 2], F32)
    sc = nc.alloc_sbuf_tensor("sc", [1, 16], F32)
    row0 = nc.alloc_sbuf_tensor("row0", [1, 2, 128], F32)
    ones_row_bf = nc.alloc_sbuf_tensor("ones_row_bf", [1, 128], BF16)
    pbfm = nc.alloc_sbuf_tensor("pbfm", [128, DC], F32)
    nwfm = nc.alloc_sbuf_tensor("nwfm", [128, DC], F32)
    rcp = nc.alloc_sbuf_tensor("rcp", [128, 8], F32)
    rb_b = nc.alloc_sbuf_tensor("rb_b", [128, E], F32)
    rbrow = nc.alloc_sbuf_tensor("rbrow", [1, E], F32)
    sfm = nc.alloc_sbuf_tensor("sfm", [128, DC], BF16)
    cfm = nc.alloc_sbuf_tensor("cfm", [128, DC], F32)
    rwf = nc.alloc_sbuf_tensor("rwf", [128, DC, E], F32)
    rwb = nc.alloc_sbuf_tensor("rwb", [128, DC, E], BF16)
    logits = nc.alloc_sbuf_tensor("logits", [128, 8, E], F32)
    lscr = nc.alloc_sbuf_tensor("lscr", [128, 4, E], F32)
    msc = nc.alloc_sbuf_tensor("msc", [128, 4], F32)
    fw = nc.alloc_sbuf_tensor("fw", [128, 8, E], F32)
    rrms_s = nc.alloc_sbuf_tensor("rrms_s", [128, 4], F32)
    bias_g = nc.alloc_sbuf_tensor("bias_g", [128, 2], F32)
    consts = nc.alloc_sbuf_tensor("consts", [128, 8], F32)

    xout = _place(nc, arena, "xout", [128, 2, D], F32, 0)
    P = [nc.alloc_psum_tensor(f"p{i}", [128, 512], F32) for i in range(4)]
    PS = nc.alloc_psum_tensor("ps4", [128, 4, 512], F32)

    def pb(i):
        return P[i][:, :] if i < 4 else PS[:, i - 4, :]

    def pbs(i, a, b):
        return P[i][:, a:b] if i < 4 else PS[:, i - 4, a:b]

    LOAD_SEMS = ["sLw0", "sLw1", "sLw2", "sLw3", "sLx0", "sLx1", "sLb0",
                 "sLb1", "sLc", "sLr", "sLrb", "sLpn", "sSt0", "sSt1",
                 "sDbg", "sDb2", "sDb3"]
    B = Sched()
    nld = {k: 0 for k in LOAD_SEMS}

    def load(stream, out, in_, waits=(), slow=False):
        B.op("sync", lambda e, o=out, i=in_, sl=slow: e.dma_start(
            out=o, in_=i, allow_slow_non_contiguous=sl),
             inc=(stream, 16), waits=waits)
        nld[stream] += 1
        return (stream, 16 * nld[stream])

    def gload(stream, out, in_, waits=(), slow=False):
        B.op("gpsimd", lambda e, o=out, i=in_, sl=slow: e.dma_start(
            out=o, in_=i, allow_slow_non_contiguous=sl),
             inc=(stream, 16), waits=waits, auto=False)
        nld[stream] += 1
        return (stream, 16 * nld[stream])

    def t_op(fn, inc=False, waits=()):
        B.op("tensor", fn, inc=("sT", 1) if inc else None, waits=waits)
        return B.n("sT")

    def v_op(fn, inc=False, waits=()):
        B.op("vector", fn, inc=("sV", 1) if inc else None, waits=waits)
        return B.n("sV")

    def s_op(fn, inc=False, waits=()):
        B.op("scalar", fn, inc=("sS", 1) if inc else None, waits=waits)
        return B.n("sS")

    def g_op(fn, inc=False, waits=()):
        B.op("gpsimd", fn, inc=("sG", 1) if inc else None, waits=waits)
        return B.n("sG")

    def mm(out, lhsT, rhs, start, stop, inc=False, waits=()):
        return t_op(lambda e, o=out, l=lhsT, r=rhs, st=start, sp=stop:
                    e.matmul(o, l, r, start=st, stop=sp), inc=inc, waits=waits)

    def phase_gate(name):
        if PHASE_LIMIT == name:
            mz = g_op(lambda e: e.memset(xout[:, :, :], 0.0),
                      waits=[("sT", B.n("sT")), ("sV", B.n("sV")),
                             ("sS", B.n("sS"))])
            for tc in range(8):
                B.op("scalar",
                     lambda e, tc=tc: e.dma_start(
                         out=bass.AP(yout, tc * 128 * D, [[D, 128], [1, D]]),
                         in_=xout[:, tc % 2, :]),
                     inc=(f"sSt{tc % 2}", 16),
                     waits=[("sG", mz), (f"sSt{tc % 2}", 16 * (tc // 2 - 1))],
                     auto=False)
            B.op("scalar", lambda e: None,
                 waits=[("sSt0", 64), ("sSt1", 64)], auto=False)
            B.frozen = True

    m1 = g_op(lambda e: e.memset(ident[:, :], 0.0), inc=True)
    m2 = g_op(lambda e: e.memset(identf[:, :], 0.0), inc=True)
    g_op(lambda e: e.affine_select(out=ident[:, :], in_=ident[:, :],
                                   compare_op=ALU.not_equal, fill=1.0, base=0,
                                   pattern=[[-1, 128]], channel_multiplier=1),
         waits=[("sG", m1)])
    g_op(lambda e: e.affine_select(out=identf[:, :], in_=identf[:, :],
                                   compare_op=ALU.not_equal, fill=1.0, base=0,
                                   pattern=[[-1, 128]], channel_multiplier=1),
         waits=[("sG", m2)])
    g_op(lambda e: e.memset(ones_row[:, :], 1.0))
    g_op(lambda e: e.memset(ones_row_bf[:, :], 1.0))
    g_op(lambda e: e.memset(ones_col_bf[:, :], 1.0))
    cm = g_op(lambda e: e.memset(consts[:, 0:1], 100000.0), inc=True)
    cm = g_op(lambda e: e.memset(consts[:, 1:2], 1000000.0), inc=True)
    cm = g_op(lambda e: e.memset(consts[:, 3:4], 128.0), inc=True)
    g_op(lambda e: e.memset(consts[:, 2:3], 1.0 / float(D)))
    g_op(lambda e: e.memset(consts[:, 4:5], 1.0 / 128.0))
    cs1 = s_op(lambda e: e.activation(consts[:, 3:4], consts[:, 3:4], AF.Sqrt),
               inc=True, waits=[("sG", cm)])
    v_op(lambda e: e.reciprocal(consts[:, 0:1], consts[:, 0:1]),
         waits=[("sG", cm)])
    v_op(lambda e: e.reciprocal(consts[:, 1:2], consts[:, 1:2]))
    cv1 = v_op(lambda e: e.reciprocal(consts[:, 3:4], consts[:, 3:4]),
               inc=True, waits=[("sS", cs1)])
    g_op(lambda e: e.memset(sc[0:1, :], 0.0))
    g_init = g_op(lambda e: e.memset(vtm[:, :, 128:132], 1.0), inc=True)

    def build_bcast(dst, col_fn, nch=DC, extra=()):
        nbk = (nch + 3) // 4
        rev = {}
        evm = {}
        mm1_last = {}
        mm2_last = {}
        for bk in range(nbk):
            cs = list(range(bk * 4, min(nch, bk * 4 + 4)))
            rb = 2 + bk % 2
            w = list(extra)
            if bk >= 2:
                w.append(("sS", rev[bk - 2]))
            for i, c in enumerate(cs):
                mm1_last[bk] = mm(PS[0:1, rb, i * 128:(i + 1) * 128],
                                  col_fn(c), identf[:, :],
                                  start=True, stop=True, inc=(i == len(cs) - 1),
                                  waits=w if i == 0 else ())
            rev[bk] = s_op(lambda e, rb=rb, bk=bk: e.copy(
                row0b[0:1, bk % 2, :], PS[0:1, rb, :]),
                inc=True, waits=[("sT", mm1_last[bk])])
            ob = bk % 2
            w2 = [("sS", rev[bk])]
            if bk >= 2:
                w2.append(("sV", evm[bk - 2]))
            for i, c in enumerate(cs):
                mm2_last[bk] = mm(PS[:, ob, i * 128:(i + 1) * 128],
                                  ones_row[0:1, :],
                                  row0b[0:1, bk % 2, i * 128:(i + 1) * 128],
                                  start=True, stop=True, inc=(i == len(cs) - 1),
                                  waits=w2 if i == 0 else ())
            evm[bk] = v_op(lambda e, ob=ob, bk=bk, cs=tuple(cs): e.tensor_copy(
                dst[:, cs[0] * 128:(cs[-1] + 1) * 128],
                PS[:, ob, 0:(len(cs)) * 128]),
                inc=True, waits=[("sT", mm2_last[bk])])
        return evm[nbk - 1]

    ld_c = load("sLc", cfm[:, :], bass.AP(cvec, 0, [[1, 128], [128, DC]]), slow=True)
    sil = s_op(lambda e: e.activation(sfm[:, :], cfm[:, :], AF.Silu),
               inc=True, waits=[ld_c])

    NJJ = (6 * D) // 2048
    wm_slot_mm = {i: None for i in range(4)}
    mod_evac = {}

    _mod_lms = [0, 0, 0, 0]

    def emit_mod_chunk(jj, dcq):
        for dc in range(dcq * 4, dcq * 4 + 4):
            sl = (jj * DC + dc) % 4
            w = []
            if wm_slot_mm[sl] is not None:
                w.append(("sT", wm_slot_mm[sl]))
            wdst = wm_bf[:, sl, :] if sl < 2 else wm_bf2[:, sl - 2, :]
            ld = gload(f"sLw{sl}", wdst,
                       bass.AP(wmod, dc * 128 * (6 * D) + jj * 2048,
                               [[6 * D, 128], [1, 2048]]), waits=w)
            w2 = [ld, ("sS", sil)]
            if dc == 0 and jj >= 1:
                w2 += [("sS", mod_evac[4 * (jj - 1) + q]) for q in range(4)]
            for q in range(4):
                _mod_lms[q] = mm(PS[0:1, q, :], sfm[:, dc:dc + 1],
                                 wdst[:, q * 512:(q + 1) * 512],
                                 start=(dc == 0), stop=(dc == DC - 1),
                                 inc=True, waits=w2 if q == 0 else ())
            wm_slot_mm[sl] = _mod_lms[3]
        if dcq == 3:
            for q in range(4):
                mod_evac[4 * jj + q] = s_op(
                    lambda e, q=q, jj=jj: e.copy(
                        mod_row[0:1, jj * 2048 + q * 512:jj * 2048 + (q + 1) * 512],
                        PS[0:1, q, :]),
                    inc=True, waits=[("sT", _mod_lms[q])])

    NTC = H1_TILES
    x_ready = {}
    h1ev_s = {}
    h1ev_v = {}
    u_free = {}
    u_prev = {}

    def emit_h1_tc(tc):
        sl = tc % 2
        bw = [x_ready[tc]]
        if tc >= 2:
            bw.append(("sV", u_prev[tc - 2]))
        if "stats" not in H1_SKIP:
            for q4 in range(4):
                v_op(lambda e, sl=sl, q4=q4: e.bn_stats(
                    bno[:, sl, q4 * 6:(q4 + 1) * 6],
                    xst[:, sl, q4 * 512:(q4 + 1) * 512]),
                    waits=bw if q4 == 0 else ())
            va = v_op(lambda e, sl=sl: e.bn_aggr(bna[:, sl, :], bno[:, sl, :]),
                      inc=True)
            sstd = s_op(lambda e, sl=sl: e.activation(
                st1[:, sl, 1:2], bna[:, sl, 1:2], AF.Sqrt, bias=consts[:, 0:1]),
                inc=True, waits=[("sV", va), ("sG", g_init), ("sV", cv1)])
            v_op(lambda e, sl=sl: e.reciprocal(st1[:, sl, 1:2], st1[:, sl, 1:2]),
                 waits=[("sS", sstd)])
        else:
            v_op(lambda e, sl=sl: e.memset(bna[:, sl, :], 0.125), waits=bw)
            va = v_op(lambda e, sl=sl: e.memset(st1[:, sl, :], 0.5), inc=True)
        uw = [("sV", va)]
        if tc >= 2:
            uw.append(("sT", u_free[tc - 2]))
        u_done = v_op(lambda e, sl=sl: e.tensor_scalar(
            u_ring[:, sl, :], xst[:, sl, :], bna[:, sl, 0:1], st1[:, sl, 1:2],
            ALU.subtract, ALU.mult), inc=True, waits=uw)
        u_prev[tc] = u_done
        if tc + 2 < NTC:
            x_ready[tc + 2] = load(
                f"sLx{sl}", xst[:, sl, :],
                bass.AP(xb, (tc + 2) * 128 * D, [[D, 128], [1, D]]),
                waits=[("sV", u_done)])
        if "trans" in H1_SKIP:
            u_free[tc] = 0
            return
        for grp in range(4):
            bank = P[grp]
            gw = [("sV", u_done)]
            if tc >= 1:
                gw.append(("sS", h1ev_s.get((tc - 1, grp), 0)))
                gw.append(("sV", h1ev_v.get((tc - 1, grp), 0)))
            lm = 0
            for j in range(4):
                dc = grp * 4 + j
                lm = mm(bank[:, j * 128:(j + 1) * 128],
                        u_ring[:, sl, dc * 128:(dc + 1) * 128],
                        ident[:, :], start=True, stop=True,
                        inc=(j == 3), waits=gw if j == 0 else ())
            if grp == 3:
                u_free[tc] = lm
            for j in range(4):
                dc = grp * 4 + j
                src = bank[:, j * 128:(j + 1) * 128]
                dst = h1[:, dc, tc * 128:(tc + 1) * 128]
                if grp % 2 == 0:
                    h1ev_s[(tc, grp)] = s_op(
                        lambda e, s=src, d=dst: e.copy(d, s),
                        inc=True, waits=[("sT", lm)])
                else:
                    h1ev_v[(tc, grp)] = v_op(
                        lambda e, s=src, d=dst: e.tensor_copy(d, s),
                        inc=True, waits=[("sT", lm)])

    units = [(jj, q) for jj in range(NJJ) for q in range(4)]
    ui = 0
    for _ in range(4):
        emit_mod_chunk(*units[ui]); ui += 1
    x_ready[0] = load("sLx0", xst[:, 0, :], bass.AP(xb, 0, [[D, 128], [1, D]]))
    x_ready[1] = load("sLx1", xst[:, 1, :], bass.AP(xb, 128 * D, [[D, 128], [1, D]]))
    for tc in range(NTC):
        emit_h1_tc(tc)
        while ui * 32 <= (tc + 1) * len(units) and ui < len(units):
            emit_mod_chunk(*units[ui]); ui += 1
    while ui < len(units):
        emit_mod_chunk(*units[ui]); ui += 1
    mod_done_s = B.n("sS")

    last_mmf = 0
    for vi in range(6):
        for dc in range(DC):
            w = [("sS", mod_done_s)] if (vi == 0 and dc == 0) else ()
            last_mmf = mm(PS[:, 0, vi * DC + dc:vi * DC + dc + 1],
                          mod_row[0:1, vi * D + dc * 128:vi * D + (dc + 1) * 128],
                          ones_row_bf[0:1, 0:1], start=True, stop=True,
                          inc=(vi == 5 and dc == DC - 1), waits=w)
    modrow_dead_t = last_mmf
    g_init2 = g_op(lambda e: e.memset(vtm[:, :, 128:132], 1.0),
                   waits=[("sT", modrow_dead_t)], inc=True)
    fmify = v_op(lambda e: e.tensor_copy(
        modfm[:, :, :], PS[:, 0, 0:96].rearrange("p (v c) -> p v c", v=6)),
        inc=True, waits=[("sT", last_mmf)])
    v_op(lambda e: e.tensor_scalar_add(modfm[:, 1, :], modfm[:, 1, :], 1.0),
         waits=[("sV", fmify)])
    v_op(lambda e: e.tensor_scalar_add(modfm[:, 4, :], modfm[:, 4, :], 1.0))
    for gi in (2, 5):
        v_op(lambda e, gi=gi: e.tensor_scalar(
            gsc[:, :], modfm[:, gi, :], 0.0, None, ALU.is_ge))
        v_op(lambda e: e.tensor_scalar(
            gsc[:, :], gsc[:, :], 2.0, -1.0, ALU.mult, ALU.add))
        v_op(lambda e, gi=gi: e.tensor_scalar_mul(
            gsc2[:, :], modfm[:, gi, :], -1.0))
        v_op(lambda e, gi=gi: e.tensor_max(
            modfm[:, gi, :], modfm[:, gi, :], gsc2[:, :]))
        v_op(lambda e, gi=gi: e.tensor_scalar_max(
            modfm[:, gi, :], modfm[:, gi, :], 0.25))
        v_op(lambda e, gi=gi: e.tensor_mul(modfm[:, gi, :], modfm[:, gi, :],
                                           gsc[:, :]))
    mod_v = v_op(lambda e: e.tensor_copy(gsc[:, 0:1], gsc[:, 0:1]), inc=True)
    PHASE_MARKS.append(("m", len(B.ops["tensor"])))
    phase_gate("m")

    ev_s_all, ev_v_all = B.n("sS"), B.n("sV")
    for dc in range(DC):
        if dc % 2 == 0:
            s_op(lambda e, dc=dc: e.activation(
                h1[:, dc, :], h1[:, dc, :], AF.Identity,
                bias=modfm[:, 0, dc:dc + 1], scale=modfm[:, 1, dc:dc + 1]),
                inc=True, waits=[("sV", mod_v), ("sV", ev_v_all)])
        else:
            v_op(lambda e, dc=dc: e.tensor_scalar(
                h1[:, dc, :], h1[:, dc, :], modfm[:, 1, dc:dc + 1],
                modfm[:, 0, dc:dc + 1], ALU.mult, ALU.add),
                inc=True, waits=[("sS", ev_s_all)])
    h1_s, h1_v = B.n("sS"), B.n("sV")
    PHASE_MARKS.append(("h1", len(B.ops["tensor"])))
    phase_gate("h1")
    if DEBUG_DUMPS:
        B.op("sync", lambda e: e.dma_start(out=dbg_h1[:, :, :], in_=h1[:, :, :]),
             waits=[("sS", h1_s), ("sV", h1_v)], inc=("sDbg", 16))
        B.op("sync", lambda e: e.dma_start(out=dbg_mod[:, :, :], in_=modfm[:, :, :]),
             waits=[("sV", mod_v)], inc=("sDbg", 16))

    exp_marks = []
    av_marks = []
    prev_av_ev = 0
    prev_grp_t = 0
    prev_grp_v = 0
    wslot_t = {0: None, 1: None, 2: None, 3: None}
    for g in range(NG):
        h1w = [("sS", h1_s), ("sV", h1_v)]
        kmm = 0
        for hf in range(2):
            w = []
            if wslot_t[hf] is not None:
                w.append(("sT", wslot_t[hf]))
            else:
                w.append(("sT", modrow_dead_t))
            ld = gload(f"sLw{hf}", qkvbf[:, hf, :],
                       bass.AP(wk_t, (hf * 8 * 128) * 1024 + g * 128,
                               [[1024, 128], [1024 * 128, 8], [1, 128]]),
                       waits=w)
            for j in range(8):
                dc = hf * 8 + j
                w = [ld] + (h1w if dc == 0 else [])
                if dc == 0 and g > 0:
                    w += [("sT", prev_grp_t), ("sV", prev_grp_v)]
                for tcb in range(8):
                    kmm = mm(pb(tcb), qkvbf[:, hf, j * 128:(j + 1) * 128],
                             h1[:, dc, tcb * 512:(tcb + 1) * 512],
                             start=(dc == 0), stop=(dc == DC - 1),
                             inc=(dc % 8 == 7 and tcb == 7),
                             waits=w if tcb == 0 else ())
            wslot_t[hf] = kmm
        k_ev = {}
        for tcb in range(8):
            k_ev[tcb] = v_op(lambda e, tcb=tcb: e.tensor_copy(
                kg[:, tcb * 512:(tcb + 1) * 512], pb(tcb)),
                inc=True, waits=[("sT", kmm)])
        vmm = 0
        vld = {}
        for hf in range(2):
            vld[hf] = gload(f"sLw{hf + 2}", qkvbf[:, hf + 2, :],
                            bass.AP(wv_t, (hf * 8 * 128) * 1024 + g * 128,
                                    [[1024, 128], [1024 * 128, 8], [1, 128]]),
                            waits=[("sT", wslot_t[hf + 2])])
        for tcc in range(32):
            w = []
            if tcc == 0:
                w = [vld[0], vld[1], ("sV", k_ev[7])]
                if g == 0:
                    w.append(("sG", g_init2))
            for dc in range(DC):
                vmm = mm(pbs(tcc // 4, (tcc % 4) * 128, (tcc % 4) * 128 + 128),
                         h1[:, dc, tcc * 128:(tcc + 1) * 128],
                         qkvbf[:, 2 + dc // 8, (dc % 8) * 128:(dc % 8) * 128 + 128],
                         start=(dc == 0), stop=(dc == DC - 1),
                         inc=(tcc == 31 and dc == DC - 1),
                         waits=w if dc == 0 else ())
        wslot_t[2] = vmm
        wslot_t[3] = vmm
        v_ev = {}
        for bk in range(8):
            v_ev[bk] = v_op(lambda e, bk=bk: e.tensor_copy(
                vtm[:, bk * 4:(bk + 1) * 4, 0:128],
                pb(bk).rearrange("p (c f) -> p c f", c=4)),
                inc=True, waits=[("sT", vmm)])
        qmm = 0
        for qt in range(4):
            sl = qt
            ld = gload(f"sLw{sl}", qkvbf[:, sl, :],
                       bass.AP(wq_t, (qt * 4 * 128) * 2048 + g * 256,
                               [[2048, 128], [2048 * 128, 4], [1, 256]]),
                       waits=[("sT", wslot_t[sl])])
            for j in range(4):
                dc = qt * 4 + j
                w = [ld]
                if dc == 0:
                    w.append(("sV", v_ev[3]))
                for h in range(2):
                    for qb in range(2):
                        qmm = mm(P[h * 2 + qb][:, :],
                                 qkvbf[:, sl, j * 256 + h * 128:j * 256 + (h + 1) * 128],
                                 h1[:, dc, qb * 512:(qb + 1) * 512],
                                 start=(dc == 0), stop=(dc == DC - 1),
                                 inc=(dc % 4 == 3 and h == 1 and qb == 1),
                                 waits=w if (h == 0 and qb == 0) else ())
            wslot_t[sl] = qmm
        q_ev = {}
        for h in range(2):
            for qb in range(2):
                q_ev[(h, qb)] = v_op(lambda e, h=h, qb=qb: e.tensor_copy(
                    qg[:, h, qb * 512:(qb + 1) * 512], P[h * 2 + qb][:, :]),
                    inc=True, waits=[("sT", qmm)])
        prev_nb = 0
        for kc in range(8):
            w_extra = []
            if kc == 0 and DEBUG_DUMPS:
                w_extra.append(("sDb3", 16 * g))
            sq = v_op(lambda e, kc=kc: e.tensor_mul(
                pt[:, 0, 0:512], kg[:, kc * 512:(kc + 1) * 512],
                kg[:, kc * 512:(kc + 1) * 512]),
                inc=True,
                waits=[("sV", k_ev[kc]), ("sT", prev_nb)] + w_extra +
                      ([("sT", B.n("sT"))] if kc == 0 else []))
            lm = mm(PS[0:1, 0, :], ones_col_bf[:, 0:1], pt[:, 0, 0:512],
                    start=True, stop=True, inc=True,
                    waits=[("sV", sq)])
            dst = sc[0:1, 0:1] if kc == 0 else sc[0:1, 2:3]
            v_op(lambda e, dst=dst: e.reduce_max(dst, PS[0:1, 0, :], AX),
                 waits=[("sT", lm)])
            if kc > 0:
                v_op(lambda e: e.tensor_max(sc[0:1, 0:1], sc[0:1, 0:1],
                                            sc[0:1, 2:3]))
            prev_nb = lm
        for qc in range(4):
            h, ch = qc // 2, qc % 2
            sq = v_op(lambda e, h=h, ch=ch: e.tensor_mul(
                pt[:, 0, 0:512], qg[:, h, ch * 512:(ch + 1) * 512],
                qg[:, h, ch * 512:(ch + 1) * 512]),
                inc=True, waits=[("sV", q_ev[(h, ch)]), ("sT", prev_nb)])
            lm = mm(PS[0:1, 0, :], ones_col_bf[:, 0:1], pt[:, 0, 0:512],
                    start=True, stop=True, inc=True, waits=[("sV", sq)])
            prev_nb = lm
            dst = sc[0:1, 1:2] if qc == 0 else sc[0:1, 2:3]
            v_op(lambda e, dst=dst: e.reduce_max(dst, PS[0:1, 0, :], AX),
                 waits=[("sT", lm)])
            if qc > 0:
                v_op(lambda e: e.tensor_max(sc[0:1, 1:2], sc[0:1, 1:2],
                                            sc[0:1, 2:3]))
        v_op(lambda e: e.tensor_mul(sc[0:1, 3:4], sc[0:1, 0:1], sc[0:1, 1:2]))
        smx = s_op(lambda e: e.activation(sc[0:1, 3:4], sc[0:1, 3:4], AF.Sqrt,
                                          scale=consts[0:1, 4:5]),
                   inc=True, waits=[("sV", B.n("sV"))])
        vneg = v_op(lambda e: e.tensor_scalar_mul(sc[0:1, 3:4], sc[0:1, 3:4],
                                                  -0.5),
                    inc=True, waits=[("sS", smx)])
        bmm = mm(PS[:, 1, 0:1], ones_row[0:1, :], sc[0:1, 3:4],
                 start=True, stop=True, inc=True, waits=[("sV", vneg)])
        bias_rdy_s = s_op(lambda e, g=g: e.copy(
            bias_g[:, g % 2:g % 2 + 1], PS[:, 1, 0:1]),
            inc=True, waits=[("sT", bmm)] + (
                [("sS", exp_marks[-1])] if exp_marks else []))
        bias_rdy = B.n("sV")
        if DEBUG_DUMPS:
            B.op("sync", lambda e, g=g: e.dma_start(
                out=dbg_ms[0:1, g * 16:(g + 1) * 16], in_=sc[0:1, 0:16]),
                waits=[("sS", bias_rdy_s)], inc=("sDb3", 16))
        for h in range(2):
            head = 2 * g + h
            for qb in range(2):
                av = 0
                pend = None
                for kp in range(16):
                    u = len(exp_marks)
                    sb = u % 2
                    w = []
                    if kp == 0:
                        w += [("sV", v_ev[7]), ("sV", q_ev[(h, qb)]),
                              ("sV", bias_rdy), ("sS", bias_rdy_s)]
                    if (h, qb) == (0, 0) and kp < 2:
                        w += [("sV", v_ev[7]), ("sV", bias_rdy),
                              ("sS", bias_rdy_s)]
                    if u >= 2:
                        w.append(("sS", exp_marks[u - 2]))
                    for i in range(2):
                        kc = 2 * kp + i
                        stm = mm(PS[:, sb * 2 + i, :],
                                 kg[:, kc * 128:(kc + 1) * 128],
                                 qg[:, h, qb * 512:(qb + 1) * 512],
                                 start=True, stop=True, inc=(i == 1),
                                 waits=w if i == 0 else ())
                    if pend is not None:
                        pu, pavw = pend
                        psb = pu % 2
                        for i in range(2):
                            kc = 2 * (pu - base_u) + i
                            for qs in range(4):
                                av = mm(P[qs][:, 0:129],
                                        pt[:, psb, i * 512 + qs * 128:i * 512 + (qs + 1) * 128],
                                        vtm[:, kc, 0:129],
                                        start=(kc == 0), stop=(kc == 31),
                                        inc=(i == 1 and qs == 3),
                                        waits=pavw if (i == 0 and qs == 0) else ())
                        av_marks.append(av)
                    else:
                        base_u = u
                    ew = [("sT", stm)]
                    if len(av_marks) >= 2:
                        ew.append(("sT", av_marks[-2]))
                    em = s_op(lambda e, sb=sb, g=g: e.activation(
                        pt[:, sb, :], PS[:, sb * 2:(sb + 1) * 2, :], AF.Exp,
                        bias=bias_g[:, g % 2:g % 2 + 1],
                        scale=consts[:, 3:4]),
                        inc=True, waits=ew)
                    exp_marks.append(em)
                    avw = [("sS", em)]
                    if kp == 0:
                        avw.append(("sV", prev_av_ev))
                        avw.append(("sV", q_ev[(1, 1)]))
                    pend = (u, avw)
                pu, pavw = pend
                psb = pu % 2
                for i in range(2):
                    kc = 2 * (pu - base_u) + i
                    for qs in range(4):
                        av = mm(P[qs][:, 0:129],
                                pt[:, psb, i * 512 + qs * 128:i * 512 + (qs + 1) * 128],
                                vtm[:, kc, 0:129],
                                start=(kc == 0), stop=(kc == 31),
                                inc=(i == 1 and qs == 3),
                                waits=pavw if (i == 0 and qs == 0) else ())
                av_marks.append(av)
                for qs in range(4):
                    v_op(lambda e, qs=qs: e.reciprocal(
                        rcp[:, qs:qs + 1], P[qs][:, 128:129]),
                        waits=[("sT", av), ("sV", prev_av_ev)])
                    prev_av_ev = v_op(
                        lambda e, qs=qs, head=head, qb=qb: e.tensor_scalar_mul(
                            attn_tm[:, qb * 4 + qs, head * 128:(head + 1) * 128],
                            P[qs][:, 0:128], rcp[:, qs:qs + 1]),
                        inc=True)
        prev_grp_t = B.n("sT")
        prev_grp_v = B.n("sV")
        PHASE_MARKS.append((f"g{g}", len(B.ops["tensor"])))
    attn_v = B.n("sV")
    if DEBUG_DUMPS:
        B.op("sync", lambda e: e.dma_start(out=dbg_att[:, :, :], in_=attn_tm[:, :, :]),
             waits=[("sV", attn_v)], inc=("sDb2", 16))
        B.op("sync", lambda e: e.dma_start(out=dbg_kg[:, :], in_=kg[:, :]),
             waits=[("sV", attn_v)], inc=("sDb2", 16))
        B.op("sync", lambda e: e.dma_start(out=dbg_qg[:, :, :], in_=qg[:, :, :]),
             waits=[("sV", attn_v)], inc=("sDb2", 16))
        B.op("sync", lambda e: e.dma_start(out=dbg_vt[:, :, :], in_=vtm[:, :, :]),
             waits=[("sV", attn_v)], inc=("sDb2", 16))

    at_ev = {}
    for qt in range(8):
        for grp in range(4):
            bank = P[grp]
            w = [("sV", attn_v)]
            if DEBUG_DUMPS:
                w.append(("sDbg", 32))
            if qt >= 1:
                w.append(("sV", at_ev[(qt - 1, grp)]))
            lm = 0
            for j in range(4):
                jc = grp * 4 + j
                lm = mm(bank[:, j * 128:(j + 1) * 128],
                        attn_tm[:, qt, jc * 128:(jc + 1) * 128],
                        ident[:, :], start=True, stop=True,
                        inc=(j == 3), waits=w if j == 0 else ())
            at_ev[(qt, grp)] = v_op(
                lambda e, b=bank, qt=qt, grp=grp: e.tensor_copy(
                    attn_fm[:, grp * 4:(grp + 1) * 4, qt * 128:(qt + 1) * 128],
                    b[:, :].rearrange("p (c f) -> p c f", c=4)),
                inc=True, waits=[("sT", lm)])
    attnfm_v = B.n("sV")
    PHASE_MARKS.append(("attn", len(B.ops["tensor"])))
    phase_gate("attn")

    gab_v = build_bcast(ga_b, lambda c: modfm[:, 2, c:c + 1],
                        extra=[("sV", attnfm_v)] +
                              ([("sDb2", 64)] if DEBUG_DUMPS else []))

    oev = {}
    h1_dead_t = prev_grp_t
    prev_half_t = 0
    for ih in range(2):
        for jc in range(DC):
            sl = jc % 2
            w = [("sT", h1_dead_t if ih == 0 else prev_half_t)] + (
                [("sDbg", 32)] if DEBUG_DUMPS else [])
            gload(f"sLw{sl}", wo_bf[:, jc, :],
                  bass.AP(wo_t, jc * 128 * 2048 + ih * 1024,
                          [[2048, 128], [1, 1024]]), waits=w)
        wo_all = [("sLw0", 16 * nld["sLw0"]), ("sLw1", 16 * nld["sLw1"])]
        xr = {}
        for tc in range(2):
            xw = [("sV", attnfm_v)] if ih == 0 else [
                ("sV", oev.get((ih - 1, tc + 6), 0))]
            if DEBUG_DUMPS and ih == 0:
                xw.append(("sDb2", 64))
            xr[tc] = load(f"sLx{tc % 2}", xst[:, tc % 2, :],
                          bass.AP(xb, tc * 128 * D, [[D, 128], [1, D]]),
                          waits=xw)
        for tc in range(8):
            b0, b1 = P[(tc % 2) * 2], P[(tc % 2) * 2 + 1]
            lm = 0
            for jc in range(DC):
                w = list(wo_all) if jc == 0 else []
                if jc == 0:
                    if ih == 0:
                        w.append(("sV", gab_v))
                        w.append(("sV", attnfm_v))
                    if tc >= 2:
                        w.append(("sV", oev[(ih, tc - 2)]))
                    elif ih == 1 and tc < 2:
                        w.append(("sV", oev[(ih - 1, tc + 6)]))
                lm = mm(b0[:, :], attn_fm[:, jc, tc * 128:(tc + 1) * 128],
                        wo_bf[:, jc, 0:512],
                        start=(jc == 0), stop=(jc == DC - 1),
                        inc=(jc == DC - 1), waits=w)
                lm = mm(b1[:, :], attn_fm[:, jc, tc * 128:(tc + 1) * 128],
                        wo_bf[:, jc, 512:1024],
                        start=(jc == 0), stop=(jc == DC - 1),
                        inc=(jc == DC - 1))
            for i, bk in enumerate((b0, b1)):
                col = ih * 1024 + i * 512
                v_op(lambda e, bk=bk, tc=tc, col=col: e.tensor_mul(
                    x2[:, tc, col:col + 512], bk[:, :], ga_b[:, col:col + 512]),
                    waits=[("sT", lm), xr[tc]])
                oev[(ih, tc)] = v_op(
                    lambda e, tc=tc, col=col: e.tensor_add(
                        x2[:, tc, col:col + 512], x2[:, tc, col:col + 512],
                        xst[:, tc % 2, col:col + 512]),
                    inc=True)
            if tc + 2 < 8:
                xr[tc + 2] = load(
                    f"sLx{tc % 2}", xst[:, tc % 2, :],
                    bass.AP(xb, (tc + 2) * 128 * D, [[D, 128], [1, D]]),
                    waits=[("sV", oev[(ih, tc)])])
        prev_half_t = B.n("sT")
    x2_v = B.n("sV")
    oproj_t = B.n("sT")
    PHASE_MARKS.append(("p3", len(B.ops["tensor"])))
    phase_gate("p3")

    h2ev_s = {}
    h2ev_v = {}
    u2_free = {}
    u2_prev = {}
    for tc in range(8):
        sl = tc % 2
        bw = [("sV", x2_v), ("sT", oproj_t)]
        if tc >= 2:
            bw.append(("sV", u2_prev[tc - 2]))
        for q4 in range(4):
            v_op(lambda e, sl=sl, tc=tc, q4=q4: e.bn_stats(
                bno[:, sl, q4 * 6:(q4 + 1) * 6],
                x2[:, tc, q4 * 512:(q4 + 1) * 512]),
                waits=bw if q4 == 0 else ())
        va = v_op(lambda e, sl=sl: e.bn_aggr(bna[:, sl, :], bno[:, sl, :]),
                  inc=True)
        sstd = s_op(lambda e, sl=sl: e.activation(
            st1[:, sl, 1:2], bna[:, sl, 1:2], AF.Sqrt, bias=consts[:, 0:1]),
            inc=True, waits=[("sV", va), ("sG", g_init)])
        v_op(lambda e, sl=sl: e.reciprocal(st1[:, sl, 1:2], st1[:, sl, 1:2]),
             waits=[("sS", sstd)])
        uw = [("sV", va)]
        if tc >= 2:
            uw.append(("sT", u2_free[tc - 2]))
        u_done = v_op(lambda e, sl=sl, tc=tc: e.tensor_scalar(
            u_ring[:, sl, :], x2[:, tc, :], bna[:, sl, 0:1], st1[:, sl, 1:2],
            ALU.subtract, ALU.mult), inc=True, waits=uw)
        u2_prev[tc] = u_done
        for grp in range(4):
            bank = P[grp]
            gw = [("sV", u_done)]
            if tc >= 1:
                gw.append(("sS", h2ev_s.get((tc - 1, grp), 0)))
                gw.append(("sV", h2ev_v.get((tc - 1, grp), 0)))
            lm = 0
            for j in range(4):
                dc = grp * 4 + j
                lm = mm(bank[:, j * 128:(j + 1) * 128],
                        u_ring[:, sl, dc * 128:(dc + 1) * 128],
                        ident[:, :], start=True, stop=True,
                        inc=(j == 3), waits=gw if j == 0 else ())
            if grp == 3:
                u2_free[tc] = lm
            for j in range(4):
                dc = grp * 4 + j
                src = bank[:, j * 128:(j + 1) * 128]
                dst = h2[:, dc, tc * 128:(tc + 1) * 128]
                if grp % 2 == 0:
                    h2ev_s[(tc, grp)] = s_op(
                        lambda e, s=src, d=dst, dc=dc: e.activation(
                            d, s, AF.Identity, bias=modfm[:, 3, dc:dc + 1],
                            scale=modfm[:, 4, dc:dc + 1]),
                        inc=True, waits=[("sT", lm)])
                else:
                    h2ev_v[(tc, grp)] = v_op(
                        lambda e, s=src, d=dst, dc=dc: e.tensor_scalar(
                            d, s, modfm[:, 4, dc:dc + 1], modfm[:, 3, dc:dc + 1],
                            ALU.mult, ALU.add),
                        inc=True, waits=[("sT", lm)])
    h2_s, h2_v = B.n("sS"), B.n("sV")

    ld_rw = gload("sLr", rwb[:, :, :],
                  bass.AP(router_w, 0, [[8, 128], [1024, DC], [1, 8]]), slow=True)
    ld_rb = load("sLrb", rbrow[:, :], bass.AP(router_b, 0, [[8, 1], [1, 8]]))
    rbm = mm(PS[:, 1, 0:8], ones_row[0:1, :], rbrow[0:1, :],
             start=True, stop=True, inc=True,
             waits=[("sLrb", 16), ("sT", oproj_t)])
    rb_v = v_op(lambda e: e.tensor_copy(rb_b[:, :], PS[:, 1, 0:8]),
                inc=True, waits=[("sT", rbm)])
    lg = 0
    PHASE_MARKS.append(("p4a", len(B.ops["tensor"])))
    phase_gate("p4a")
    for tc in range(8):
        lm = 0
        for dc in range(DC):
            w = ()
            if dc == 0:
                w = [("sLr", 16), ("sS", h2_s), ("sV", h2_v), ("sV", lg)]
            lm = mm(PS[:, 0, 0:8], h2[:, dc, tc * 128:(tc + 1) * 128],
                    rwb[:, dc, :], start=(dc == 0), stop=(dc == DC - 1),
                    inc=(dc == DC - 1), waits=w)
        lg = v_op(lambda e, tc=tc: e.tensor_add(
            logits[:, tc, :], PS[:, 0, 0:8], rb_b[:, :]),
            inc=True, waits=[("sT", lm), ("sV", rb_v)])
        v_op(lambda e, tc=tc: e.reduce_max(msc[:, 0:1], logits[:, tc, :], AX),
             waits=[("sV", lg)])
        v_op(lambda e, tc=tc: e.tensor_scalar(
            lscr[:, 2, :], logits[:, tc, :], msc[:, 0:1], None, ALU.is_equal))
        v_op(lambda e, tc=tc: e.scalar_tensor_tensor(
            lscr[:, 0, :], lscr[:, 2, :], NEG_BIG, logits[:, tc, :],
            ALU.mult, ALU.add))
        v_op(lambda e: e.reduce_max(msc[:, 1:2], lscr[:, 0, :], AX))
        v_op(lambda e: e.tensor_scalar(
            lscr[:, 3, :], lscr[:, 0, :], msc[:, 1:2], None, ALU.is_equal))
        vd = v_op(lambda e: e.tensor_sub(msc[:, 2:3], msc[:, 0:1], msc[:, 1:2]),
                  inc=True)
        sw1 = s_op(lambda e: e.activation(msc[:, 3:4], msc[:, 2:3], AF.Sigmoid),
                   inc=True, waits=[("sV", vd)])
        v_op(lambda e: e.tensor_scalar(
            msc[:, 2:3], msc[:, 3:4], -1.0, 1.0, ALU.mult, ALU.add),
            waits=[("sS", sw1)])
        v_op(lambda e: e.tensor_scalar_mul(lscr[:, 3, :], lscr[:, 3, :],
                                           msc[:, 2:3]))
        v_op(lambda e, tc=tc: e.scalar_tensor_tensor(
            fw[:, tc, :], lscr[:, 2, :], msc[:, 3:4], lscr[:, 3, :],
            ALU.mult, ALU.add), inc=True)
    router_v = B.n("sV")

    we_slot_t = {i: None for i in range(4)}
    wbc_mark = {0: None, 1: None}
    silu_marks = {}
    silu_last = {}
    comb_v = {}
    for e_i in range(E):
        sl = e_i % 2
        wb = [("sV", router_v)]
        if wbc_mark[sl] is not None:
            wb.append(("sV", wbc_mark[sl]))
        build_bcast(wbc[:, sl, :], lambda c, e_i=e_i: fw[:, c, e_i:e_i + 1],
                    nch=8, extra=wb)
        wbc_rdy = B.n("sV")
        ld_be = load(f"sLb{sl}", befm[:, sl, :],
                     bass.AP(expert_b, e_i * D, [[1, 128], [128, DC]]),
                     waits=[("sS", silu_marks.get(e_i - 2, 0))], slow=True)
        for ic in range(DC):
            ws = (e_i * DC + ic) % 4
            cw = []
            if we_slot_t[ws] is not None:
                cw.append(("sT", we_slot_t[ws]))
            else:
                cw.append(("sT", oproj_t))
            wld = gload(f"sLw{ws}", webf[:, ws, :, :],
                        bass.AP(expert_w, e_i * D * D + ic * 128,
                                [[D, 128], [128 * D, 16], [1, 128]]),
                        waits=cw)
            lm = 0
            for dc in range(DC):
                w = [wld]
                if dc == 0:
                    pk = (e_i, ic - 2) if ic >= 2 else (e_i - 1, DC + ic - 2)
                    w.append(("sS", silu_last.get(pk, 0)))
                    w.append(("sV", wbc_rdy))
                for tcb in range(2):
                    lm = mm(P[ic % 2 * 2 + tcb][:, :], webf[:, ws, dc, :],
                            h2[:, dc, tcb * 512:(tcb + 1) * 512],
                            start=(dc == 0), stop=(dc == DC - 1),
                            inc=(dc == DC - 1),
                            waits=w if (dc == 0 and tcb == 0) else ())
            we_slot_t[ws] = lm
            if ic > 0:
                prevc = comb_v[(e_i, ic - 1)]
            elif e_i > 0:
                prevc = comb_v[(e_i - 1, DC - 1)]
            else:
                prevc = 0
            silu_last[(e_i, ic)] = 0
            for tcb in range(2):
                sm = s_op(lambda e, ic=ic, tcb=tcb, sl=sl, e_i=e_i: e.activation(
                    silu_r[:, tcb, :], P[ic % 2 * 2 + tcb][:, :], AF.Silu,
                    bias=befm[:, sl, ic:ic + 1]),
                    inc=True, waits=[("sT", lm), ld_be, ("sV", prevc)])
                silu_marks[e_i] = sm
                silu_last[(e_i, ic)] = sm
                if e_i == 0:
                    comb_v[(e_i, ic)] = v_op(
                        lambda e, ic=ic, tcb=tcb, sl=sl: e.tensor_mul(
                            combined[:, ic, tcb * 512:(tcb + 1) * 512],
                            silu_r[:, tcb, :], wbc[:, sl, tcb * 512:(tcb + 1) * 512]),
                        inc=True, waits=[("sS", sm), ("sV", wbc_rdy),
                                         ("sV", prevc)])
                else:
                    v_op(lambda e, ic=ic, tcb=tcb, sl=sl: e.tensor_mul(
                        tmp_bf[:, tcb, :], silu_r[:, tcb, :],
                        wbc[:, sl, tcb * 512:(tcb + 1) * 512]),
                        waits=[("sS", sm), ("sV", wbc_rdy), ("sV", prevc)])
                    comb_v[(e_i, ic)] = v_op(
                        lambda e, ic=ic, tcb=tcb: e.tensor_add(
                            combined[:, ic, tcb * 512:(tcb + 1) * 512],
                            combined[:, ic, tcb * 512:(tcb + 1) * 512],
                            tmp_bf[:, tcb, :]),
                        inc=True)
        wbc_mark[sl] = B.n("sV")
    comb_done_v = B.n("sV")
    PHASE_MARKS.append(("p5", len(B.ops["tensor"])))
    phase_gate("p5")

    pj_ev = {}
    pslab_free = {}
    experts_t = B.n("sT")
    for fc in range(4):
        for dq in range(8):
            sl = dq % 2
            cw = []
            if fc > 0:
                cw.append(("sT", pslab_free[fc - 1]))
            else:
                cw.append(("sT", experts_t))
            gload(f"sLw{sl}", pslab[:, dq * 2:(dq + 1) * 2, :],
                  bass.AP(proj_w, (dq * 2 * 128) * D + fc * 512,
                          [[D, 128], [128 * D, 2], [1, 512]]), waits=cw)
        pj_all = [("sLw0", 16 * nld["sLw0"]), ("sLw1", 16 * nld["sLw1"])]
        for tc in range(8):
            bank = P[tc % 4]
            lm = 0
            for dc in range(DC):
                w = []
                if dc == 0:
                    w = pj_all + [("sV", comb_done_v)]
                    if tc >= 4:
                        w.append(("sV", pj_ev[(fc, tc - 4)]))
                    elif fc > 0:
                        w.append(("sV", pj_ev[(fc - 1, tc + 4)]))
                lm = mm(bank[:, :], combined[:, dc, tc * 128:(tc + 1) * 128],
                        pslab[:, dc, :], start=(dc == 0), stop=(dc == DC - 1),
                        inc=(dc == DC - 1), waits=w if dc == 0 else ())
            pj_ev[(fc, tc)] = v_op(
                lambda e, bank=bank, tc=tc, fc=fc: e.tensor_copy(
                    proj_out[:, tc, fc * 512:(fc + 1) * 512], bank[:, :]),
                inc=True, waits=[("sT", lm)])
        pslab_free[fc] = B.n("sT")
    proj_v = B.n("sV")
    proj_t = B.n("sT")
    PHASE_MARKS.append(("p6a", len(B.ops["tensor"])))
    phase_gate("p6a")

    ld_pb = load("sLpn", pbfm[:, :], bass.AP(proj_b, 0, [[1, 128], [128, DC]]),
                 slow=True)
    ld_nw = load("sLpn", nwfm[:, :], bass.AP(norm_w, 0, [[1, 128], [128, DC]]),
                 slow=True)
    pbb_v = build_bcast(pb_b, lambda c: pbfm[:, c:c + 1],
                        extra=[("sLpn", 32), ("sT", proj_t)])
    s3fm = v_op(lambda e: e.tensor_mul(nwfm[:, :], nwfm[:, :],
                                       modfm[:, 5, :]),
                inc=True, waits=[("sLpn", 32)])
    s3b_v = build_bcast(s3_b, lambda c: nwfm[:, c:c + 1],
                        extra=[("sV", s3fm)])

    st_marks = {}
    last_v6 = {}
    h2tm_done = {}
    pj_done = B.n("sT")
    for tc in range(8):
        sl = tc % 2
        hv = []
        for grp in range(4):
            w = [("sV", s3b_v), ("sV", proj_v), ("sT", pj_done),
                 ("sV", last_v6.get(tc - 2, 0))]
            if grp >= 2:
                w.append(("sS", hv[grp - 2]))
            elif tc >= 2:
                w.append(("sS", h2tm_done[tc - 2]))
            lm = 0
            for j in range(4):
                dc = grp * 4 + j
                lm = mm(PS[:, sl * 2 + grp % 2, j * 128:(j + 1) * 128],
                        h2[:, dc, tc * 128:(tc + 1) * 128],
                        ident[:, :], start=True, stop=True,
                        inc=(j == 3), waits=w if j == 0 else ())
            hv.append(s_op(lambda e, grp=grp, sl=sl: e.copy(
                h2tm[:, sl, grp * 512:(grp + 1) * 512],
                PS[:, sl * 2 + grp % 2, :]),
                inc=True, waits=[("sT", lm)]))
        h2tm_done[tc] = hv[-1]
        v1 = v_op(lambda e, tc=tc, sl=sl: e.scalar_tensor_tensor(
            y_buf[:, sl, :], proj_out[:, tc, :], 1.0, h2tm[:, sl, :],
            ALU.mult, ALU.add), inc=True, waits=[("sS", hv[-1])])
        v2 = v_op(lambda e, sl=sl: e.tensor_add(h2tm[:, sl, :], y_buf[:, sl, :],
                                                pb_b[:, :]),
                  inc=True, waits=[("sV", v1)])
        v_op(lambda e, sl=sl: e.tensor_mul(y_buf[:, sl, :], h2tm[:, sl, :],
                                           h2tm[:, sl, :]),
             waits=[("sV", v2)])
        v3 = v_op(lambda e, sl=sl: e.reduce_sum(rrms_s[:, sl * 2:sl * 2 + 1],
                                                y_buf[:, sl, :], AX),
                  inc=True)
        s1 = s_op(lambda e, sl=sl: e.activation(
            rrms_s[:, sl * 2 + 1:sl * 2 + 2], rrms_s[:, sl * 2:sl * 2 + 1],
            AF.Sqrt, bias=consts[:, 1:2], scale=consts[:, 2:3]),
            inc=True, waits=[("sV", v3)])
        v4 = v_op(lambda e, sl=sl: e.reciprocal(
            rrms_s[:, sl * 2 + 1:sl * 2 + 2], rrms_s[:, sl * 2 + 1:sl * 2 + 2]),
            inc=True, waits=[("sS", s1)])
        s2 = s_op(lambda e, sl=sl: e.activation(
            y_buf[:, sl, :], h2tm[:, sl, :], AF.Identity,
            scale=rrms_s[:, sl * 2 + 1:sl * 2 + 2]),
            inc=True, waits=[("sV", v4)])
        v5 = v_op(lambda e, sl=sl: e.tensor_mul(h2tm[:, sl, :], y_buf[:, sl, :],
                                                s3_b[:, :]),
                  waits=[("sS", s2)])
        xw = [(f"sSt{sl}", st_marks.get(tc - 2, 0))]
        v6 = v_op(lambda e, tc=tc, sl=sl: e.tensor_add(
            xout[:, sl, :], h2tm[:, sl, :], x2[:, tc, :]),
            inc=True, waits=xw)
        last_v6[tc] = v6
        B.op("scalar",
             lambda e, tc=tc, sl=sl: e.dma_start(
                 out=bass.AP(yout, tc * 128 * D, [[D, 128], [1, D]]),
                 in_=xout[:, sl, :]),
             inc=(f"sSt{sl}", 16), waits=[("sV", v6)], auto=False)
        st_marks[tc] = 16 * (tc // 2 + 1)
    B.op("scalar", lambda e: None,
         waits=[("sSt0", 64), ("sSt1", 64)] +
               ([("sDb2", 64)] if DEBUG_DUMPS else []),
         auto=False)

    from contextlib import ExitStack
    with ExitStack() as stack:
        sems = {}
        for nm in LOAD_SEMS + ["sT", "sV", "sS", "sG"]:
            sems[nm] = stack.enter_context(nc.semaphore(nm))
        block = stack.enter_context(nc.Block())

        def run(name):
            def body(eng_obj):
                for fn, inc, waits in B.ops[name]:
                    for s, v in waits:
                        eng_obj.wait_ge(sems[s], v)
                    ins = fn(eng_obj)
                    if inc is not None:
                        s, k = inc
                        assert ins is not None, (name, s)
                        ins.then_inc(sems[s], k)
            return body
        block.sync(run("sync"))
        block.tensor(run("tensor"))
        block.vector(run("vector"))
        block.scalar(run("scalar"))
        block.gpsimd(run("gpsimd"))
    return nc


_NC_CACHE = {}


def _get_nc():
    if "nc" not in _NC_CACHE:
        _NC_CACHE["nc"] = build_program()
    return _NC_CACHE["nc"]


def kernel(L0_attn_hidden_states, L0_attn_wq, L0_attn_wk, L0_attn_wv,
           L0_attn_wo, L0_ffn_router_w, L0_ffn_router_b, L0_ffn_expert_w,
           L0_ffn_expert_b, L0_ffn_proj_w, L0_ffn_proj_b, L0_ffn_norm_w,
           L0_c, L0_wmod):
    x = np.ascontiguousarray(np.asarray(L0_attn_hidden_states, dtype=np.float32))
    wq_t = np.ascontiguousarray(np.asarray(L0_attn_wq, np.float32).T)
    wk_t = np.ascontiguousarray(np.asarray(L0_attn_wk, np.float32).T)
    wv_t = np.ascontiguousarray(np.asarray(L0_attn_wv, np.float32).T)
    wo_t = np.ascontiguousarray(np.asarray(L0_attn_wo, np.float32).T)
    router_w = np.ascontiguousarray(np.asarray(L0_ffn_router_w, np.float32))
    router_b = np.asarray(L0_ffn_router_b, np.float32).reshape(1, E)
    expert_w = np.ascontiguousarray(np.asarray(L0_ffn_expert_w, np.float32))
    expert_b = np.ascontiguousarray(np.asarray(L0_ffn_expert_b, np.float32))
    proj_w = np.ascontiguousarray(np.asarray(L0_ffn_proj_w, np.float32))
    proj_b = np.asarray(L0_ffn_proj_b, np.float32).reshape(1, D)
    norm_w = np.asarray(L0_ffn_norm_w, np.float32).reshape(1, D)
    cmat = np.asarray(L0_c, np.float32)
    wmod = np.ascontiguousarray(np.asarray(L0_wmod, np.float32))

    nc = _get_nc()
    in_maps = []
    for c in range(8):
        b, r0 = c // 4, 1024 * (c % 4)
        xbc = np.ascontiguousarray(np.roll(x[b], -r0, axis=0))
        in_maps.append({
            "xb": xbc, "wq_t": wq_t, "wk_t": wk_t, "wv_t": wv_t, "wo_t": wo_t,
            "router_w": router_w, "router_b": router_b, "expert_w": expert_w,
            "expert_b": expert_b, "proj_w": proj_w, "proj_b": proj_b,
            "norm_w": norm_w, "cvec": cmat[b:b + 1], "wmod": wmod,
        })
    res = run_bass_kernel_spmd(nc, in_maps, core_ids=list(range(8)))
    out = np.empty((2, S, D), np.float32)
    for c in range(8):
        b, r0 = c // 4, 1024 * (c % 4)
        out[b, r0:r0 + 1024] = res.results[c]["yout"]
    return out


# version 15
# speedup vs baseline: 1.0271x; 1.0271x over previous
import numpy as np

DEBUG_DUMPS = False
PHASE_MARKS = []
PHASE_LIMIT = None
H1_TILES = 32
H1_SKIP = set()

import concourse.bass as bass
import concourse.mybir as mybir
from concourse.bass_utils import run_bass_kernel_spmd

F32 = mybir.dt.float32
BF16 = mybir.dt.bfloat16
U8 = mybir.dt.uint8
AF = mybir.ActivationFunctionType
ALU = mybir.AluOpType
AX = mybir.AxisListType.X

D = 2048
DC = 16
S = 4096
T = 1024
NG = 8
E = 8
INV_SQRT_HD = float(1.0 / np.sqrt(128.0))
LN_EPS = 1e-5
RMS_EPS = 1e-6
KB = 1024
NEG_BIG = -1e30


def _place(nc, backing, name, shape, dtype, off, span=None):
    per_part = int(np.prod(shape[1:])) * mybir.dt.size(dtype)
    limit = span if span is not None else int(np.prod(backing.shape[1:])) * mybir.dt.size(backing.dtype)
    assert off + per_part <= limit, (name, off, per_part, limit)
    base = nc.lookup_mloc(backing).addr
    return nc.alloc_sbuf_tensor_at(name, list(shape), dtype, offset=base + off)


class Sched:

    ENGINES = ("sync", "tensor", "vector", "scalar", "gpsimd")
    OWN = {"vector": "sV", "scalar": "sS", "gpsimd": "sG"}

    def __init__(self):
        self.ops = {e: [] for e in self.ENGINES}
        self.cnt = {}
        self.lastwait = {e: {} for e in self.ENGINES}
        self.frozen = False

    def n(self, sem):
        return self.cnt.get(sem, 0)

    def op(self, eng, fn, inc=None, waits=(), auto=True):
        if self.frozen:
            return self.n(inc[0]) if inc else None
        own = self.OWN.get(eng)
        waits = list(waits)
        if own is not None and auto:
            waits.append((own, self.n(own)))
            if inc is None:
                inc = (own, 1)
        real = []
        for s, v in waits:
            if v is None or v <= 0:
                continue
            if self.lastwait[eng].get(s, 0) >= v:
                continue
            self.lastwait[eng][s] = v
            real.append((s, v))
        self.ops[eng].append((fn, inc, tuple(real)))
        if inc is not None:
            s, k = inc
            self.cnt[s] = self.cnt.get(s, 0) + k
        return self.n(inc[0]) if inc else None


def build_program():
    nc = bass.Bass("TRN2", target_bir_lowering=False)

    xb = nc.dram_tensor("xb", [S, D], F32, kind="ExternalInput")
    wq_t = nc.dram_tensor("wq_t", [D, 2048], F32, kind="ExternalInput")
    wk_t = nc.dram_tensor("wk_t", [D, 1024], F32, kind="ExternalInput")
    wv_t = nc.dram_tensor("wv_t", [D, 1024], F32, kind="ExternalInput")
    wo_t = nc.dram_tensor("wo_t", [D, 2048], F32, kind="ExternalInput")
    router_w = nc.dram_tensor("router_w", [D, E], F32, kind="ExternalInput")
    router_b = nc.dram_tensor("router_b", [1, E], F32, kind="ExternalInput")
    expert_w = nc.dram_tensor("expert_w", [E, D, D], F32, kind="ExternalInput")
    expert_b = nc.dram_tensor("expert_b", [E, D], F32, kind="ExternalInput")
    proj_w = nc.dram_tensor("proj_w", [D, D], F32, kind="ExternalInput")
    proj_b = nc.dram_tensor("proj_b", [1, D], F32, kind="ExternalInput")
    norm_w = nc.dram_tensor("norm_w", [1, D], F32, kind="ExternalInput")
    cvec = nc.dram_tensor("cvec", [1, D], F32, kind="ExternalInput")
    wmod = nc.dram_tensor("wmod", [D, 6 * D], F32, kind="ExternalInput")
    yout = nc.dram_tensor("yout", [T, D], F32, kind="ExternalOutput")
    if DEBUG_DUMPS:
        dbg_h1 = nc.dram_tensor("dbg_h1", [128, DC, S], BF16, kind="ExternalOutput")
        dbg_att = nc.dram_tensor("dbg_att", [128, 8, D], BF16, kind="ExternalOutput")
        dbg_mod = nc.dram_tensor("dbg_mod", [128, 6, DC], F32, kind="ExternalOutput")
        dbg_kg = nc.dram_tensor("dbg_kg", [128, S], BF16, kind="ExternalOutput")
        dbg_qg = nc.dram_tensor("dbg_qg", [128, 2, T], BF16, kind="ExternalOutput")
        dbg_vt = nc.dram_tensor("dbg_vt", [128, 32, 132], BF16, kind="ExternalOutput")
        dbg_ms = nc.dram_tensor("dbg_ms", [1, NG * 16], F32, kind="ExternalOutput")

    arena = nc.alloc_sbuf_tensor("arena", [128, 128 * KB], U8)
    r1 = nc.alloc_sbuf_tensor("r1", [128, 32 * KB], U8)
    r245 = nc.alloc_sbuf_tensor("r245", [128, 32 * KB], U8)
    r3 = nc.alloc_sbuf_tensor("r3", [128, 8 * KB], U8)

    mod_row = _place(nc, r1, "mod_row", [1, 6 * D], BF16, 0)
    h1 = _place(nc, arena, "h1", [128, DC, S], BF16, 0)
    attn_fm = _place(nc, arena, "attn_fm", [128, DC, T], BF16, 0)
    combined = _place(nc, arena, "combined", [128, DC, T], BF16, 0)
    pb_b = _place(nc, r245, "pb_b", [128, D], F32, 16 * KB)
    s3_b = _place(nc, arena, "s3_b", [128, D], F32, 24 * KB)
    h2 = _place(nc, arena, "h2", [128, DC, T], BF16, 32 * KB)
    wo_bf = _place(nc, arena, "wo_bf", [128, DC, 1024], BF16, 32 * KB)
    x2 = _place(nc, arena, "x2", [128, 8, D], F32, 64 * KB)

    kg = _place(nc, r1, "kg", [128, S], BF16, 0)
    vtm = _place(nc, r1, "vtm", [128, 32, 132], BF16, 8 * KB)
    qg = _place(nc, r1, "qg", [128, 2, T], BF16, 17 * KB)
    qkvbf = _place(nc, r1, "qkvbf", [128, 4, 1024], BF16, 21 * KB)
    ga_b = _place(nc, r1, "ga_b", [128, D], F32, 0)
    wbc = _place(nc, r1, "wbc", [128, 2, T], F32, 0)
    tmp_bf = _place(nc, r1, "tmp_bf", [128, 2, 512], BF16, 8 * KB)
    silu_r = _place(nc, r1, "silu_r", [128, 2, 512], BF16, 10 * KB)
    webf = _place(nc, r1, "webf", [128, 4, 16, 128], BF16, 12 * KB)
    befm = _place(nc, r1, "befm", [128, 2, DC], F32, 28 * KB)
    proj_out = _place(nc, r1, "proj_out", [128, 8, D], BF16, 0)

    xst = _place(nc, r245, "xst", [128, 2, D], F32, 0)
    pslab = _place(nc, r245, "pslab", [128, DC, 512], BF16, 0)
    attn_tm = _place(nc, r245, "attn_tm", [128, 8, D], BF16, 0)
    u_ring = _place(nc, r245, "u_ring", [128, 2, D], BF16, 16 * KB)
    h2tm = _place(nc, r245, "h2tm", [128, 2, D], F32, 0)
    y_buf = _place(nc, r245, "y_buf", [128, 2, D], F32, 16 * KB)

    wm_bf = _place(nc, r3, "wm_bf", [128, 2, 2048], BF16, 0)
    wm_bf2 = _place(nc, r245, "wm_bf2", [128, 2, 2048], BF16, 24 * KB)
    pt = _place(nc, r3, "pt", [128, 2, 1024], BF16, 0)
    row0b = _place(nc, r3, "row0b", [1, 2, 512], F32, 4 * KB)
    rowb = _place(nc, r3, "rowb", [1, 512], F32, 6 * KB)

    ident = nc.alloc_sbuf_tensor("ident", [128, 128], BF16)
    identf = nc.alloc_sbuf_tensor("identf", [128, 128], F32)
    ones_row = nc.alloc_sbuf_tensor("ones_row", [1, 128], F32)
    ones_col_bf = nc.alloc_sbuf_tensor("ones_col_bf", [128, 1], BF16)
    modfm = nc.alloc_sbuf_tensor("modfm", [128, 6, DC], F32)
    gsc = nc.alloc_sbuf_tensor("gsc", [128, DC], F32)
    gsc2 = nc.alloc_sbuf_tensor("gsc2", [128, DC], F32)
    bno = nc.alloc_sbuf_tensor("bno", [128, 2, 24], F32)
    bna = nc.alloc_sbuf_tensor("bna", [128, 2, 2], F32)
    st1 = nc.alloc_sbuf_tensor("st1", [128, 2, 2], F32)
    sc = nc.alloc_sbuf_tensor("sc", [1, 16], F32)
    row0 = nc.alloc_sbuf_tensor("row0", [1, 2, 128], F32)
    ones_row_bf = nc.alloc_sbuf_tensor("ones_row_bf", [1, 128], BF16)
    pbfm = nc.alloc_sbuf_tensor("pbfm", [128, DC], F32)
    nwfm = nc.alloc_sbuf_tensor("nwfm", [128, DC], F32)
    rcp = nc.alloc_sbuf_tensor("rcp", [128, 8], F32)
    rb_b = nc.alloc_sbuf_tensor("rb_b", [128, E], F32)
    rbrow = nc.alloc_sbuf_tensor("rbrow", [1, E], F32)
    sfm = nc.alloc_sbuf_tensor("sfm", [128, DC], BF16)
    cfm = nc.alloc_sbuf_tensor("cfm", [128, DC], F32)
    rwf = nc.alloc_sbuf_tensor("rwf", [128, DC, E], F32)
    rwb = nc.alloc_sbuf_tensor("rwb", [128, DC, E], BF16)
    logits = nc.alloc_sbuf_tensor("logits", [128, 8, E], F32)
    lscr = nc.alloc_sbuf_tensor("lscr", [128, 4, E], F32)
    msc = nc.alloc_sbuf_tensor("msc", [128, 4], F32)
    fw = nc.alloc_sbuf_tensor("fw", [128, 8, E], F32)
    rrms_s = nc.alloc_sbuf_tensor("rrms_s", [128, 4], F32)
    bias_g = nc.alloc_sbuf_tensor("bias_g", [128, 2], F32)
    consts = nc.alloc_sbuf_tensor("consts", [128, 8], F32)

    xout = _place(nc, arena, "xout", [128, 2, D], F32, 0)
    P = [nc.alloc_psum_tensor(f"p{i}", [128, 512], F32) for i in range(4)]
    PS = nc.alloc_psum_tensor("ps4", [128, 4, 512], F32)

    def pb(i):
        return P[i][:, :] if i < 4 else PS[:, i - 4, :]

    def pbs(i, a, b):
        return P[i][:, a:b] if i < 4 else PS[:, i - 4, a:b]

    LOAD_SEMS = ["sLw0", "sLw1", "sLw2", "sLw3", "sLx0", "sLx1", "sLb0",
                 "sLb1", "sLc", "sLr", "sLrb", "sLpn", "sSt0", "sSt1",
                 "sDbg", "sDb2", "sDb3"]
    B = Sched()
    nld = {k: 0 for k in LOAD_SEMS}

    def load(stream, out, in_, waits=(), slow=False):
        B.op("sync", lambda e, o=out, i=in_, sl=slow: e.dma_start(
            out=o, in_=i, allow_slow_non_contiguous=sl),
             inc=(stream, 16), waits=waits)
        nld[stream] += 1
        return (stream, 16 * nld[stream])

    def gload(stream, out, in_, waits=(), slow=False):
        B.op("gpsimd", lambda e, o=out, i=in_, sl=slow: e.dma_start(
            out=o, in_=i, allow_slow_non_contiguous=sl),
             inc=(stream, 16), waits=waits, auto=False)
        nld[stream] += 1
        return (stream, 16 * nld[stream])

    def t_op(fn, inc=False, waits=()):
        B.op("tensor", fn, inc=("sT", 1) if inc else None, waits=waits)
        return B.n("sT")

    def v_op(fn, inc=False, waits=()):
        B.op("vector", fn, inc=("sV", 1) if inc else None, waits=waits)
        return B.n("sV")

    def s_op(fn, inc=False, waits=()):
        B.op("scalar", fn, inc=("sS", 1) if inc else None, waits=waits)
        return B.n("sS")

    def g_op(fn, inc=False, waits=()):
        B.op("gpsimd", fn, inc=("sG", 1) if inc else None, waits=waits)
        return B.n("sG")

    def mm(out, lhsT, rhs, start, stop, inc=False, waits=()):
        return t_op(lambda e, o=out, l=lhsT, r=rhs, st=start, sp=stop:
                    e.matmul(o, l, r, start=st, stop=sp), inc=inc, waits=waits)

    def phase_gate(name):
        if PHASE_LIMIT == name:
            mz = g_op(lambda e: e.memset(xout[:, :, :], 0.0),
                      waits=[("sT", B.n("sT")), ("sV", B.n("sV")),
                             ("sS", B.n("sS"))])
            for tc in range(8):
                B.op("scalar",
                     lambda e, tc=tc: e.dma_start(
                         out=bass.AP(yout, tc * 128 * D, [[D, 128], [1, D]]),
                         in_=xout[:, tc % 2, :]),
                     inc=(f"sSt{tc % 2}", 16),
                     waits=[("sG", mz), (f"sSt{tc % 2}", 16 * (tc // 2 - 1))],
                     auto=False)
            B.op("scalar", lambda e: None,
                 waits=[("sSt0", 64), ("sSt1", 64)], auto=False)
            B.frozen = True

    m1 = g_op(lambda e: e.memset(ident[:, :], 0.0), inc=True)
    m2 = g_op(lambda e: e.memset(identf[:, :], 0.0), inc=True)
    g_op(lambda e: e.affine_select(out=ident[:, :], in_=ident[:, :],
                                   compare_op=ALU.not_equal, fill=1.0, base=0,
                                   pattern=[[-1, 128]], channel_multiplier=1),
         waits=[("sG", m1)])
    g_op(lambda e: e.affine_select(out=identf[:, :], in_=identf[:, :],
                                   compare_op=ALU.not_equal, fill=1.0, base=0,
                                   pattern=[[-1, 128]], channel_multiplier=1),
         waits=[("sG", m2)])
    g_op(lambda e: e.memset(ones_row[:, :], 1.0))
    g_op(lambda e: e.memset(ones_row_bf[:, :], 1.0))
    g_op(lambda e: e.memset(ones_col_bf[:, :], 1.0))
    cm = g_op(lambda e: e.memset(consts[:, 0:1], 100000.0), inc=True)
    cm = g_op(lambda e: e.memset(consts[:, 1:2], 1000000.0), inc=True)
    cm = g_op(lambda e: e.memset(consts[:, 3:4], 128.0), inc=True)
    g_op(lambda e: e.memset(consts[:, 2:3], 1.0 / float(D)))
    g_op(lambda e: e.memset(consts[:, 4:5], 1.0 / 128.0))
    cs1 = s_op(lambda e: e.activation(consts[:, 3:4], consts[:, 3:4], AF.Sqrt),
               inc=True, waits=[("sG", cm)])
    v_op(lambda e: e.reciprocal(consts[:, 0:1], consts[:, 0:1]),
         waits=[("sG", cm)])
    v_op(lambda e: e.reciprocal(consts[:, 1:2], consts[:, 1:2]))
    cv1 = v_op(lambda e: e.reciprocal(consts[:, 3:4], consts[:, 3:4]),
               inc=True, waits=[("sS", cs1)])
    g_op(lambda e: e.memset(sc[0:1, :], 0.0))
    g_init = g_op(lambda e: e.memset(vtm[:, :, 128:132], 1.0), inc=True)

    def build_bcast(dst, col_fn, nch=DC, extra=()):
        nbk = (nch + 3) // 4
        rev = {}
        evm = {}
        mm1_last = {}
        mm2_last = {}
        for bk in range(nbk):
            cs = list(range(bk * 4, min(nch, bk * 4 + 4)))
            rb = 2 + bk % 2
            w = list(extra)
            if bk >= 2:
                w.append(("sS", rev[bk - 2]))
            for i, c in enumerate(cs):
                mm1_last[bk] = mm(PS[0:1, rb, i * 128:(i + 1) * 128],
                                  col_fn(c), identf[:, :],
                                  start=True, stop=True, inc=(i == len(cs) - 1),
                                  waits=w if i == 0 else ())
            rev[bk] = s_op(lambda e, rb=rb, bk=bk: e.copy(
                row0b[0:1, bk % 2, :], PS[0:1, rb, :]),
                inc=True, waits=[("sT", mm1_last[bk])])
            ob = bk % 2
            w2 = [("sS", rev[bk])]
            if bk >= 2:
                w2.append(("sV", evm[bk - 2]))
            for i, c in enumerate(cs):
                mm2_last[bk] = mm(PS[:, ob, i * 128:(i + 1) * 128],
                                  ones_row[0:1, :],
                                  row0b[0:1, bk % 2, i * 128:(i + 1) * 128],
                                  start=True, stop=True, inc=(i == len(cs) - 1),
                                  waits=w2 if i == 0 else ())
            evm[bk] = v_op(lambda e, ob=ob, bk=bk, cs=tuple(cs): e.tensor_copy(
                dst[:, cs[0] * 128:(cs[-1] + 1) * 128],
                PS[:, ob, 0:(len(cs)) * 128]),
                inc=True, waits=[("sT", mm2_last[bk])])
        return evm[nbk - 1]

    ld_c = load("sLc", cfm[:, :], bass.AP(cvec, 0, [[1, 128], [128, DC]]), slow=True)
    sil = s_op(lambda e: e.activation(sfm[:, :], cfm[:, :], AF.Silu),
               inc=True, waits=[ld_c])

    NJJ = (6 * D) // 2048
    wm_slot_mm = {i: None for i in range(4)}
    mod_evac = {}

    _mod_lms = [0, 0, 0, 0]

    def emit_mod_chunk(jj, dcq):
        for dc in range(dcq * 4, dcq * 4 + 4):
            sl = (jj * DC + dc) % 4
            w = []
            if wm_slot_mm[sl] is not None:
                w.append(("sT", wm_slot_mm[sl]))
            wdst = wm_bf[:, sl, :] if sl < 2 else wm_bf2[:, sl - 2, :]
            ld = gload(f"sLw{sl}", wdst,
                       bass.AP(wmod, dc * 128 * (6 * D) + jj * 2048,
                               [[6 * D, 128], [1, 2048]]), waits=w)
            w2 = [ld, ("sS", sil)]
            if dc == 0 and jj >= 1:
                w2 += [("sS", mod_evac[4 * (jj - 1) + q]) for q in range(4)]
            for q in range(4):
                _mod_lms[q] = mm(PS[0:1, q, :], sfm[:, dc:dc + 1],
                                 wdst[:, q * 512:(q + 1) * 512],
                                 start=(dc == 0), stop=(dc == DC - 1),
                                 inc=True, waits=w2 if q == 0 else ())
            wm_slot_mm[sl] = _mod_lms[3]
        if dcq == 3:
            for q in range(4):
                mod_evac[4 * jj + q] = s_op(
                    lambda e, q=q, jj=jj: e.copy(
                        mod_row[0:1, jj * 2048 + q * 512:jj * 2048 + (q + 1) * 512],
                        PS[0:1, q, :]),
                    inc=True, waits=[("sT", _mod_lms[q])])

    NTC = H1_TILES
    x_ready = {}
    h1ev_s = {}
    h1ev_v = {}
    u_free = {}
    u_prev = {}

    def emit_h1_tc(tc):
        sl = tc % 2
        bw = [x_ready[tc]]
        if tc >= 2:
            bw.append(("sV", u_prev[tc - 2]))
        if "stats" not in H1_SKIP:
            for q4 in range(4):
                v_op(lambda e, sl=sl, q4=q4: e.bn_stats(
                    bno[:, sl, q4 * 6:(q4 + 1) * 6],
                    xst[:, sl, q4 * 512:(q4 + 1) * 512]),
                    waits=bw if q4 == 0 else ())
            va = v_op(lambda e, sl=sl: e.bn_aggr(bna[:, sl, :], bno[:, sl, :]),
                      inc=True)
            sstd = s_op(lambda e, sl=sl: e.activation(
                st1[:, sl, 1:2], bna[:, sl, 1:2], AF.Sqrt, bias=consts[:, 0:1]),
                inc=True, waits=[("sV", va), ("sG", g_init), ("sV", cv1)])
            v_op(lambda e, sl=sl: e.reciprocal(st1[:, sl, 1:2], st1[:, sl, 1:2]),
                 waits=[("sS", sstd)])
        else:
            v_op(lambda e, sl=sl: e.memset(bna[:, sl, :], 0.125), waits=bw)
            va = v_op(lambda e, sl=sl: e.memset(st1[:, sl, :], 0.5), inc=True)
        uw = [("sV", va)]
        if tc >= 2:
            uw.append(("sT", u_free[tc - 2]))
        u_done = v_op(lambda e, sl=sl: e.tensor_scalar(
            u_ring[:, sl, :], xst[:, sl, :], bna[:, sl, 0:1], st1[:, sl, 1:2],
            ALU.subtract, ALU.mult), inc=True, waits=uw)
        u_prev[tc] = u_done
        if tc + 2 < NTC:
            x_ready[tc + 2] = load(
                f"sLx{sl}", xst[:, sl, :],
                bass.AP(xb, (tc + 2) * 128 * D, [[D, 128], [1, D]]),
                waits=[("sV", u_done)])
        if "trans" in H1_SKIP:
            u_free[tc] = 0
            return
        for grp in range(4):
            bank = P[grp]
            gw = [("sV", u_done)]
            if tc >= 1:
                gw.append(("sS", h1ev_s.get((tc - 1, grp), 0)))
                gw.append(("sV", h1ev_v.get((tc - 1, grp), 0)))
            lm = 0
            for j in range(4):
                dc = grp * 4 + j
                lm = mm(bank[:, j * 128:(j + 1) * 128],
                        u_ring[:, sl, dc * 128:(dc + 1) * 128],
                        ident[:, :], start=True, stop=True,
                        inc=(j == 3), waits=gw if j == 0 else ())
            if grp == 3:
                u_free[tc] = lm
            for j in range(4):
                dc = grp * 4 + j
                src = bank[:, j * 128:(j + 1) * 128]
                dst = h1[:, dc, tc * 128:(tc + 1) * 128]
                if grp % 2 == 0:
                    h1ev_s[(tc, grp)] = s_op(
                        lambda e, s=src, d=dst: e.copy(d, s),
                        inc=True, waits=[("sT", lm)])
                else:
                    h1ev_v[(tc, grp)] = v_op(
                        lambda e, s=src, d=dst: e.tensor_copy(d, s),
                        inc=True, waits=[("sT", lm)])

    units = [(jj, q) for jj in range(NJJ) for q in range(4)]
    ui = 0
    for _ in range(4):
        emit_mod_chunk(*units[ui]); ui += 1
    x_ready[0] = load("sLx0", xst[:, 0, :], bass.AP(xb, 0, [[D, 128], [1, D]]))
    x_ready[1] = load("sLx1", xst[:, 1, :], bass.AP(xb, 128 * D, [[D, 128], [1, D]]))
    for tc in range(NTC):
        emit_h1_tc(tc)
        while ui * 32 <= (tc + 1) * len(units) and ui < len(units):
            emit_mod_chunk(*units[ui]); ui += 1
    while ui < len(units):
        emit_mod_chunk(*units[ui]); ui += 1
    mod_done_s = B.n("sS")

    last_mmf = 0
    for vi in range(6):
        for dc in range(DC):
            w = [("sS", mod_done_s)] if (vi == 0 and dc == 0) else ()
            last_mmf = mm(PS[:, 0, vi * DC + dc:vi * DC + dc + 1],
                          mod_row[0:1, vi * D + dc * 128:vi * D + (dc + 1) * 128],
                          ones_row_bf[0:1, 0:1], start=True, stop=True,
                          inc=(vi == 5 and dc == DC - 1), waits=w)
    modrow_dead_t = last_mmf
    g_init2 = g_op(lambda e: e.memset(vtm[:, :, 128:132], 1.0),
                   waits=[("sT", modrow_dead_t)], inc=True)
    fmify = v_op(lambda e: e.tensor_copy(
        modfm[:, :, :], PS[:, 0, 0:96].rearrange("p (v c) -> p v c", v=6)),
        inc=True, waits=[("sT", last_mmf)])
    v_op(lambda e: e.tensor_scalar_add(modfm[:, 1, :], modfm[:, 1, :], 1.0),
         waits=[("sV", fmify)])
    v_op(lambda e: e.tensor_scalar_add(modfm[:, 4, :], modfm[:, 4, :], 1.0))
    for gi in (2, 5):
        v_op(lambda e, gi=gi: e.tensor_scalar(
            gsc[:, :], modfm[:, gi, :], 0.0, None, ALU.is_ge))
        v_op(lambda e: e.tensor_scalar(
            gsc[:, :], gsc[:, :], 2.0, -1.0, ALU.mult, ALU.add))
        v_op(lambda e, gi=gi: e.tensor_scalar_mul(
            gsc2[:, :], modfm[:, gi, :], -1.0))
        v_op(lambda e, gi=gi: e.tensor_max(
            modfm[:, gi, :], modfm[:, gi, :], gsc2[:, :]))
        v_op(lambda e, gi=gi: e.tensor_scalar_max(
            modfm[:, gi, :], modfm[:, gi, :], 0.25))
        v_op(lambda e, gi=gi: e.tensor_mul(modfm[:, gi, :], modfm[:, gi, :],
                                           gsc[:, :]))
    mod_v = v_op(lambda e: e.tensor_copy(gsc[:, 0:1], gsc[:, 0:1]), inc=True)
    PHASE_MARKS.append(("m", len(B.ops["tensor"])))
    phase_gate("m")

    ev_s_all, ev_v_all = B.n("sS"), B.n("sV")
    for dc in range(DC):
        if dc % 2 == 0:
            s_op(lambda e, dc=dc: e.activation(
                h1[:, dc, :], h1[:, dc, :], AF.Identity,
                bias=modfm[:, 0, dc:dc + 1], scale=modfm[:, 1, dc:dc + 1]),
                inc=True, waits=[("sV", mod_v), ("sV", ev_v_all)])
        else:
            v_op(lambda e, dc=dc: e.tensor_scalar(
                h1[:, dc, :], h1[:, dc, :], modfm[:, 1, dc:dc + 1],
                modfm[:, 0, dc:dc + 1], ALU.mult, ALU.add),
                inc=True, waits=[("sS", ev_s_all)])
    h1_s, h1_v = B.n("sS"), B.n("sV")
    PHASE_MARKS.append(("h1", len(B.ops["tensor"])))
    phase_gate("h1")
    if DEBUG_DUMPS:
        B.op("sync", lambda e: e.dma_start(out=dbg_h1[:, :, :], in_=h1[:, :, :]),
             waits=[("sS", h1_s), ("sV", h1_v)], inc=("sDbg", 16))
        B.op("sync", lambda e: e.dma_start(out=dbg_mod[:, :, :], in_=modfm[:, :, :]),
             waits=[("sV", mod_v)], inc=("sDbg", 16))

    exp_marks = []
    av_marks = []
    prev_av_ev = 0
    prev_grp_t = 0
    prev_grp_v = 0
    wslot_t = {0: None, 1: None, 2: None, 3: None}
    for g in range(NG):
        h1w = [("sS", h1_s), ("sV", h1_v)]
        kmm = 0
        for hf in range(2):
            w = []
            if wslot_t[hf] is not None:
                w.append(("sT", wslot_t[hf]))
            else:
                w.append(("sT", modrow_dead_t))
            ld = gload(f"sLw{hf}", qkvbf[:, hf, :],
                       bass.AP(wk_t, (hf * 8 * 128) * 1024 + g * 128,
                               [[1024, 128], [1024 * 128, 8], [1, 128]]),
                       waits=w)
            for j in range(8):
                dc = hf * 8 + j
                w = [ld] + (h1w if dc == 0 else [])
                if dc == 0 and g > 0:
                    w += [("sT", prev_grp_t), ("sV", prev_grp_v)]
                for tcb in range(8):
                    kmm = mm(pb(tcb), qkvbf[:, hf, j * 128:(j + 1) * 128],
                             h1[:, dc, tcb * 512:(tcb + 1) * 512],
                             start=(dc == 0), stop=(dc == DC - 1),
                             inc=(dc % 8 == 7 and tcb == 7),
                             waits=w if tcb == 0 else ())
            wslot_t[hf] = kmm
        k_ev = {}
        for tcb in range(8):
            k_ev[tcb] = v_op(lambda e, tcb=tcb: e.tensor_copy(
                kg[:, tcb * 512:(tcb + 1) * 512], pb(tcb)),
                inc=True, waits=[("sT", kmm)])
        vmm = 0
        vld = {}
        for hf in range(2):
            vld[hf] = gload(f"sLw{hf + 2}", qkvbf[:, hf + 2, :],
                            bass.AP(wv_t, (hf * 8 * 128) * 1024 + g * 128,
                                    [[1024, 128], [1024 * 128, 8], [1, 128]]),
                            waits=[("sT", wslot_t[hf + 2])])
        for tcc in range(32):
            w = []
            if tcc % 4 == 0:
                w = [("sV", k_ev[tcc // 4])]
                if tcc == 0:
                    w += [vld[0], vld[1]]
                    if g == 0:
                        w.append(("sG", g_init2))
            for dc in range(DC):
                vmm = mm(pbs(tcc // 4, (tcc % 4) * 128, (tcc % 4) * 128 + 128),
                         h1[:, dc, tcc * 128:(tcc + 1) * 128],
                         qkvbf[:, 2 + dc // 8, (dc % 8) * 128:(dc % 8) * 128 + 128],
                         start=(dc == 0), stop=(dc == DC - 1),
                         inc=(tcc % 4 == 3 and dc == DC - 1),
                         waits=w if dc == 0 else ())
        wslot_t[2] = vmm
        wslot_t[3] = vmm
        v_ev = {}
        for bk in range(8):
            v_ev[bk] = v_op(lambda e, bk=bk: e.tensor_copy(
                vtm[:, bk * 4:(bk + 1) * 4, 0:128],
                pb(bk).rearrange("p (c f) -> p c f", c=4)),
                inc=True, waits=[("sT", vmm)])
        qmm = 0
        for qt in range(4):
            sl = qt
            ld = gload(f"sLw{sl}", qkvbf[:, sl, :],
                       bass.AP(wq_t, (qt * 4 * 128) * 2048 + g * 256,
                               [[2048, 128], [2048 * 128, 4], [1, 256]]),
                       waits=[("sT", wslot_t[sl])])
            for j in range(4):
                dc = qt * 4 + j
                w = [ld]
                if dc == 0:
                    w.append(("sV", v_ev[3]))
                for h in range(2):
                    for qb in range(2):
                        qmm = mm(P[h * 2 + qb][:, :],
                                 qkvbf[:, sl, j * 256 + h * 128:j * 256 + (h + 1) * 128],
                                 h1[:, dc, qb * 512:(qb + 1) * 512],
                                 start=(dc == 0), stop=(dc == DC - 1),
                                 inc=(dc % 4 == 3 and h == 1 and qb == 1),
                                 waits=w if (h == 0 and qb == 0) else ())
            wslot_t[sl] = qmm
        q_ev = {}
        for h in range(2):
            for qb in range(2):
                q_ev[(h, qb)] = v_op(lambda e, h=h, qb=qb: e.tensor_copy(
                    qg[:, h, qb * 512:(qb + 1) * 512], P[h * 2 + qb][:, :]),
                    inc=True, waits=[("sT", qmm)])
        prev_nb = 0
        for kc in range(8):
            w_extra = []
            if kc == 0 and DEBUG_DUMPS:
                w_extra.append(("sDb3", 16 * g))
            sq = v_op(lambda e, kc=kc: e.tensor_mul(
                pt[:, 0, 0:512], kg[:, kc * 512:(kc + 1) * 512],
                kg[:, kc * 512:(kc + 1) * 512]),
                inc=True,
                waits=[("sV", k_ev[kc]), ("sT", prev_nb)] + w_extra +
                      ([("sT", B.n("sT"))] if kc == 0 else []))
            lm = mm(PS[0:1, 0, :], ones_col_bf[:, 0:1], pt[:, 0, 0:512],
                    start=True, stop=True, inc=True,
                    waits=[("sV", sq)])
            dst = sc[0:1, 0:1] if kc == 0 else sc[0:1, 2:3]
            v_op(lambda e, dst=dst: e.reduce_max(dst, PS[0:1, 0, :], AX),
                 waits=[("sT", lm)])
            if kc > 0:
                v_op(lambda e: e.tensor_max(sc[0:1, 0:1], sc[0:1, 0:1],
                                            sc[0:1, 2:3]))
            prev_nb = lm
        for qc in range(4):
            h, ch = qc // 2, qc % 2
            sq = v_op(lambda e, h=h, ch=ch: e.tensor_mul(
                pt[:, 0, 0:512], qg[:, h, ch * 512:(ch + 1) * 512],
                qg[:, h, ch * 512:(ch + 1) * 512]),
                inc=True, waits=[("sV", q_ev[(h, ch)]), ("sT", prev_nb)])
            lm = mm(PS[0:1, 0, :], ones_col_bf[:, 0:1], pt[:, 0, 0:512],
                    start=True, stop=True, inc=True, waits=[("sV", sq)])
            prev_nb = lm
            dst = sc[0:1, 1:2] if qc == 0 else sc[0:1, 2:3]
            v_op(lambda e, dst=dst: e.reduce_max(dst, PS[0:1, 0, :], AX),
                 waits=[("sT", lm)])
            if qc > 0:
                v_op(lambda e: e.tensor_max(sc[0:1, 1:2], sc[0:1, 1:2],
                                            sc[0:1, 2:3]))
        v_op(lambda e: e.tensor_mul(sc[0:1, 3:4], sc[0:1, 0:1], sc[0:1, 1:2]))
        smx = s_op(lambda e: e.activation(sc[0:1, 3:4], sc[0:1, 3:4], AF.Sqrt,
                                          scale=consts[0:1, 4:5]),
                   inc=True, waits=[("sV", B.n("sV"))])
        vneg = v_op(lambda e: e.tensor_scalar_mul(sc[0:1, 3:4], sc[0:1, 3:4],
                                                  -0.5),
                    inc=True, waits=[("sS", smx)])
        bmm = mm(PS[:, 1, 0:1], ones_row[0:1, :], sc[0:1, 3:4],
                 start=True, stop=True, inc=True, waits=[("sV", vneg)])
        bias_rdy_s = s_op(lambda e, g=g: e.copy(
            bias_g[:, g % 2:g % 2 + 1], PS[:, 1, 0:1]),
            inc=True, waits=[("sT", bmm)] + (
                [("sS", exp_marks[-1])] if exp_marks else []))
        bias_rdy = B.n("sV")
        if DEBUG_DUMPS:
            B.op("sync", lambda e, g=g: e.dma_start(
                out=dbg_ms[0:1, g * 16:(g + 1) * 16], in_=sc[0:1, 0:16]),
                waits=[("sS", bias_rdy_s)], inc=("sDb3", 16))
        for h in range(2):
            head = 2 * g + h
            for qb in range(2):
                av = 0
                pend = None
                for kp in range(16):
                    u = len(exp_marks)
                    sb = u % 2
                    w = []
                    if kp == 0:
                        w += [("sV", v_ev[7]), ("sV", q_ev[(h, qb)]),
                              ("sV", bias_rdy), ("sS", bias_rdy_s)]
                    if (h, qb) == (0, 0) and kp < 2:
                        w += [("sV", v_ev[7]), ("sV", bias_rdy),
                              ("sS", bias_rdy_s)]
                    if u >= 2:
                        w.append(("sS", exp_marks[u - 2]))
                    for i in range(2):
                        kc = 2 * kp + i
                        stm = mm(PS[:, sb * 2 + i, :],
                                 kg[:, kc * 128:(kc + 1) * 128],
                                 qg[:, h, qb * 512:(qb + 1) * 512],
                                 start=True, stop=True, inc=(i == 1),
                                 waits=w if i == 0 else ())
                    if pend is not None:
                        pu, pavw = pend
                        psb = pu % 2
                        for i in range(2):
                            kc = 2 * (pu - base_u) + i
                            for qs in range(4):
                                av = mm(P[qs][:, 0:129],
                                        pt[:, psb, i * 512 + qs * 128:i * 512 + (qs + 1) * 128],
                                        vtm[:, kc, 0:129],
                                        start=(kc == 0), stop=(kc == 31),
                                        inc=(i == 1 and qs == 3),
                                        waits=pavw if (i == 0 and qs == 0) else ())
                        av_marks.append(av)
                    else:
                        base_u = u
                    ew = [("sT", stm)]
                    if len(av_marks) >= 2:
                        ew.append(("sT", av_marks[-2]))
                    em = s_op(lambda e, sb=sb, g=g: e.activation(
                        pt[:, sb, :], PS[:, sb * 2:(sb + 1) * 2, :], AF.Exp,
                        bias=bias_g[:, g % 2:g % 2 + 1],
                        scale=consts[:, 3:4]),
                        inc=True, waits=ew)
                    exp_marks.append(em)
                    avw = [("sS", em)]
                    if kp == 0:
                        avw.append(("sV", prev_av_ev))
                        avw.append(("sV", q_ev[(1, 1)]))
                    pend = (u, avw)
                pu, pavw = pend
                psb = pu % 2
                for i in range(2):
                    kc = 2 * (pu - base_u) + i
                    for qs in range(4):
                        av = mm(P[qs][:, 0:129],
                                pt[:, psb, i * 512 + qs * 128:i * 512 + (qs + 1) * 128],
                                vtm[:, kc, 0:129],
                                start=(kc == 0), stop=(kc == 31),
                                inc=(i == 1 and qs == 3),
                                waits=pavw if (i == 0 and qs == 0) else ())
                av_marks.append(av)
                for qs in range(4):
                    v_op(lambda e, qs=qs: e.reciprocal(
                        rcp[:, qs:qs + 1], P[qs][:, 128:129]),
                        waits=[("sT", av), ("sV", prev_av_ev)])
                    prev_av_ev = v_op(
                        lambda e, qs=qs, head=head, qb=qb: e.tensor_scalar_mul(
                            attn_tm[:, qb * 4 + qs, head * 128:(head + 1) * 128],
                            P[qs][:, 0:128], rcp[:, qs:qs + 1]),
                        inc=True)
        prev_grp_t = B.n("sT")
        prev_grp_v = B.n("sV")
        PHASE_MARKS.append((f"g{g}", len(B.ops["tensor"])))
    attn_v = B.n("sV")
    if DEBUG_DUMPS:
        B.op("sync", lambda e: e.dma_start(out=dbg_att[:, :, :], in_=attn_tm[:, :, :]),
             waits=[("sV", attn_v)], inc=("sDb2", 16))
        B.op("sync", lambda e: e.dma_start(out=dbg_kg[:, :], in_=kg[:, :]),
             waits=[("sV", attn_v)], inc=("sDb2", 16))
        B.op("sync", lambda e: e.dma_start(out=dbg_qg[:, :, :], in_=qg[:, :, :]),
             waits=[("sV", attn_v)], inc=("sDb2", 16))
        B.op("sync", lambda e: e.dma_start(out=dbg_vt[:, :, :], in_=vtm[:, :, :]),
             waits=[("sV", attn_v)], inc=("sDb2", 16))

    at_ev = {}
    for qt in range(8):
        for grp in range(4):
            bank = P[grp]
            w = [("sV", attn_v)]
            if DEBUG_DUMPS:
                w.append(("sDbg", 32))
            if qt >= 1:
                w.append(("sV", at_ev[(qt - 1, grp)]))
            lm = 0
            for j in range(4):
                jc = grp * 4 + j
                lm = mm(bank[:, j * 128:(j + 1) * 128],
                        attn_tm[:, qt, jc * 128:(jc + 1) * 128],
                        ident[:, :], start=True, stop=True,
                        inc=(j == 3), waits=w if j == 0 else ())
            at_ev[(qt, grp)] = v_op(
                lambda e, b=bank, qt=qt, grp=grp: e.tensor_copy(
                    attn_fm[:, grp * 4:(grp + 1) * 4, qt * 128:(qt + 1) * 128],
                    b[:, :].rearrange("p (c f) -> p c f", c=4)),
                inc=True, waits=[("sT", lm)])
    attnfm_v = B.n("sV")
    PHASE_MARKS.append(("attn", len(B.ops["tensor"])))
    phase_gate("attn")

    gab_v = build_bcast(ga_b, lambda c: modfm[:, 2, c:c + 1],
                        extra=[("sV", attnfm_v)] +
                              ([("sDb2", 64)] if DEBUG_DUMPS else []))

    oev = {}
    h1_dead_t = prev_grp_t
    prev_half_t = 0
    for ih in range(2):
        for jc in range(DC):
            sl = jc % 2
            w = [("sT", h1_dead_t if ih == 0 else prev_half_t)] + (
                [("sDbg", 32)] if DEBUG_DUMPS else [])
            gload(f"sLw{sl}", wo_bf[:, jc, :],
                  bass.AP(wo_t, jc * 128 * 2048 + ih * 1024,
                          [[2048, 128], [1, 1024]]), waits=w)
        wo_all = [("sLw0", 16 * nld["sLw0"]), ("sLw1", 16 * nld["sLw1"])]
        xr = {}
        for tc in range(2):
            xw = [("sV", attnfm_v)] if ih == 0 else [
                ("sV", oev.get((ih - 1, tc + 6), 0))]
            if DEBUG_DUMPS and ih == 0:
                xw.append(("sDb2", 64))
            xr[tc] = load(f"sLx{tc % 2}", xst[:, tc % 2, :],
                          bass.AP(xb, tc * 128 * D, [[D, 128], [1, D]]),
                          waits=xw)
        for tc in range(8):
            b0, b1 = P[(tc % 2) * 2], P[(tc % 2) * 2 + 1]
            lm = 0
            for jc in range(DC):
                w = list(wo_all) if jc == 0 else []
                if jc == 0:
                    if ih == 0:
                        w.append(("sV", gab_v))
                        w.append(("sV", attnfm_v))
                    if tc >= 2:
                        w.append(("sV", oev[(ih, tc - 2)]))
                    elif ih == 1 and tc < 2:
                        w.append(("sV", oev[(ih - 1, tc + 6)]))
                lm = mm(b0[:, :], attn_fm[:, jc, tc * 128:(tc + 1) * 128],
                        wo_bf[:, jc, 0:512],
                        start=(jc == 0), stop=(jc == DC - 1),
                        inc=(jc == DC - 1), waits=w)
                lm = mm(b1[:, :], attn_fm[:, jc, tc * 128:(tc + 1) * 128],
                        wo_bf[:, jc, 512:1024],
                        start=(jc == 0), stop=(jc == DC - 1),
                        inc=(jc == DC - 1))
            for i, bk in enumerate((b0, b1)):
                col = ih * 1024 + i * 512
                v_op(lambda e, bk=bk, tc=tc, col=col: e.tensor_mul(
                    x2[:, tc, col:col + 512], bk[:, :], ga_b[:, col:col + 512]),
                    waits=[("sT", lm), xr[tc]])
                oev[(ih, tc)] = v_op(
                    lambda e, tc=tc, col=col: e.tensor_add(
                        x2[:, tc, col:col + 512], x2[:, tc, col:col + 512],
                        xst[:, tc % 2, col:col + 512]),
                    inc=True)
            if tc + 2 < 8:
                xr[tc + 2] = load(
                    f"sLx{tc % 2}", xst[:, tc % 2, :],
                    bass.AP(xb, (tc + 2) * 128 * D, [[D, 128], [1, D]]),
                    waits=[("sV", oev[(ih, tc)])])
        prev_half_t = B.n("sT")
    x2_v = B.n("sV")
    oproj_t = B.n("sT")
    PHASE_MARKS.append(("p3", len(B.ops["tensor"])))
    phase_gate("p3")

    h2ev_s = {}
    h2ev_v = {}
    u2_free = {}
    u2_prev = {}
    for tc in range(8):
        sl = tc % 2
        bw = [("sV", x2_v), ("sT", oproj_t)]
        if tc >= 2:
            bw.append(("sV", u2_prev[tc - 2]))
        for q4 in range(4):
            v_op(lambda e, sl=sl, tc=tc, q4=q4: e.bn_stats(
                bno[:, sl, q4 * 6:(q4 + 1) * 6],
                x2[:, tc, q4 * 512:(q4 + 1) * 512]),
                waits=bw if q4 == 0 else ())
        va = v_op(lambda e, sl=sl: e.bn_aggr(bna[:, sl, :], bno[:, sl, :]),
                  inc=True)
        sstd = s_op(lambda e, sl=sl: e.activation(
            st1[:, sl, 1:2], bna[:, sl, 1:2], AF.Sqrt, bias=consts[:, 0:1]),
            inc=True, waits=[("sV", va), ("sG", g_init)])
        v_op(lambda e, sl=sl: e.reciprocal(st1[:, sl, 1:2], st1[:, sl, 1:2]),
             waits=[("sS", sstd)])
        uw = [("sV", va)]
        if tc >= 2:
            uw.append(("sT", u2_free[tc - 2]))
        u_done = v_op(lambda e, sl=sl, tc=tc: e.tensor_scalar(
            u_ring[:, sl, :], x2[:, tc, :], bna[:, sl, 0:1], st1[:, sl, 1:2],
            ALU.subtract, ALU.mult), inc=True, waits=uw)
        u2_prev[tc] = u_done
        for grp in range(4):
            bank = P[grp]
            gw = [("sV", u_done)]
            if tc >= 1:
                gw.append(("sS", h2ev_s.get((tc - 1, grp), 0)))
                gw.append(("sV", h2ev_v.get((tc - 1, grp), 0)))
            lm = 0
            for j in range(4):
                dc = grp * 4 + j
                lm = mm(bank[:, j * 128:(j + 1) * 128],
                        u_ring[:, sl, dc * 128:(dc + 1) * 128],
                        ident[:, :], start=True, stop=True,
                        inc=(j == 3), waits=gw if j == 0 else ())
            if grp == 3:
                u2_free[tc] = lm
            for j in range(4):
                dc = grp * 4 + j
                src = bank[:, j * 128:(j + 1) * 128]
                dst = h2[:, dc, tc * 128:(tc + 1) * 128]
                if grp % 2 == 0:
                    h2ev_s[(tc, grp)] = s_op(
                        lambda e, s=src, d=dst, dc=dc: e.activation(
                            d, s, AF.Identity, bias=modfm[:, 3, dc:dc + 1],
                            scale=modfm[:, 4, dc:dc + 1]),
                        inc=True, waits=[("sT", lm)])
                else:
                    h2ev_v[(tc, grp)] = v_op(
                        lambda e, s=src, d=dst, dc=dc: e.tensor_scalar(
                            d, s, modfm[:, 4, dc:dc + 1], modfm[:, 3, dc:dc + 1],
                            ALU.mult, ALU.add),
                        inc=True, waits=[("sT", lm)])
    h2_s, h2_v = B.n("sS"), B.n("sV")

    ld_rw = gload("sLr", rwb[:, :, :],
                  bass.AP(router_w, 0, [[8, 128], [1024, DC], [1, 8]]), slow=True)
    ld_rb = load("sLrb", rbrow[:, :], bass.AP(router_b, 0, [[8, 1], [1, 8]]))
    rbm = mm(PS[:, 1, 0:8], ones_row[0:1, :], rbrow[0:1, :],
             start=True, stop=True, inc=True,
             waits=[("sLrb", 16), ("sT", oproj_t)])
    rb_v = v_op(lambda e: e.tensor_copy(rb_b[:, :], PS[:, 1, 0:8]),
                inc=True, waits=[("sT", rbm)])
    lg = 0
    PHASE_MARKS.append(("p4a", len(B.ops["tensor"])))
    phase_gate("p4a")
    for tc in range(8):
        lm = 0
        for dc in range(DC):
            w = ()
            if dc == 0:
                w = [("sLr", 16), ("sS", h2_s), ("sV", h2_v), ("sV", lg)]
            lm = mm(PS[:, 0, 0:8], h2[:, dc, tc * 128:(tc + 1) * 128],
                    rwb[:, dc, :], start=(dc == 0), stop=(dc == DC - 1),
                    inc=(dc == DC - 1), waits=w)
        lg = v_op(lambda e, tc=tc: e.tensor_add(
            logits[:, tc, :], PS[:, 0, 0:8], rb_b[:, :]),
            inc=True, waits=[("sT", lm), ("sV", rb_v)])
        v_op(lambda e, tc=tc: e.reduce_max(msc[:, 0:1], logits[:, tc, :], AX),
             waits=[("sV", lg)])
        v_op(lambda e, tc=tc: e.tensor_scalar(
            lscr[:, 2, :], logits[:, tc, :], msc[:, 0:1], None, ALU.is_equal))
        v_op(lambda e, tc=tc: e.scalar_tensor_tensor(
            lscr[:, 0, :], lscr[:, 2, :], NEG_BIG, logits[:, tc, :],
            ALU.mult, ALU.add))
        v_op(lambda e: e.reduce_max(msc[:, 1:2], lscr[:, 0, :], AX))
        v_op(lambda e: e.tensor_scalar(
            lscr[:, 3, :], lscr[:, 0, :], msc[:, 1:2], None, ALU.is_equal))
        vd = v_op(lambda e: e.tensor_sub(msc[:, 2:3], msc[:, 0:1], msc[:, 1:2]),
                  inc=True)
        sw1 = s_op(lambda e: e.activation(msc[:, 3:4], msc[:, 2:3], AF.Sigmoid),
                   inc=True, waits=[("sV", vd)])
        v_op(lambda e: e.tensor_scalar(
            msc[:, 2:3], msc[:, 3:4], -1.0, 1.0, ALU.mult, ALU.add),
            waits=[("sS", sw1)])
        v_op(lambda e: e.tensor_scalar_mul(lscr[:, 3, :], lscr[:, 3, :],
                                           msc[:, 2:3]))
        v_op(lambda e, tc=tc: e.scalar_tensor_tensor(
            fw[:, tc, :], lscr[:, 2, :], msc[:, 3:4], lscr[:, 3, :],
            ALU.mult, ALU.add), inc=True)
    router_v = B.n("sV")

    we_slot_t = {i: None for i in range(4)}
    wbc_mark = {0: None, 1: None}
    silu_marks = {}
    silu_last = {}
    comb_v = {}
    for e_i in range(E):
        sl = e_i % 2
        wb = [("sV", router_v)]
        if wbc_mark[sl] is not None:
            wb.append(("sV", wbc_mark[sl]))
        build_bcast(wbc[:, sl, :], lambda c, e_i=e_i: fw[:, c, e_i:e_i + 1],
                    nch=8, extra=wb)
        wbc_rdy = B.n("sV")
        ld_be = load(f"sLb{sl}", befm[:, sl, :],
                     bass.AP(expert_b, e_i * D, [[1, 128], [128, DC]]),
                     waits=[("sS", silu_marks.get(e_i - 2, 0))], slow=True)
        for ic in range(DC):
            ws = (e_i * DC + ic) % 4
            cw = []
            if we_slot_t[ws] is not None:
                cw.append(("sT", we_slot_t[ws]))
            else:
                cw.append(("sT", oproj_t))
            wld = gload(f"sLw{ws}", webf[:, ws, :, :],
                        bass.AP(expert_w, e_i * D * D + ic * 128,
                                [[D, 128], [128 * D, 16], [1, 128]]),
                        waits=cw)
            lm = 0
            for dc in range(DC):
                w = [wld]
                if dc == 0:
                    pk = (e_i, ic - 2) if ic >= 2 else (e_i - 1, DC + ic - 2)
                    w.append(("sS", silu_last.get(pk, 0)))
                    w.append(("sV", wbc_rdy))
                for tcb in range(2):
                    lm = mm(P[ic % 2 * 2 + tcb][:, :], webf[:, ws, dc, :],
                            h2[:, dc, tcb * 512:(tcb + 1) * 512],
                            start=(dc == 0), stop=(dc == DC - 1),
                            inc=(dc == DC - 1),
                            waits=w if (dc == 0 and tcb == 0) else ())
            we_slot_t[ws] = lm
            if ic > 0:
                prevc = comb_v[(e_i, ic - 1)]
            elif e_i > 0:
                prevc = comb_v[(e_i - 1, DC - 1)]
            else:
                prevc = 0
            silu_last[(e_i, ic)] = 0
            for tcb in range(2):
                sm = s_op(lambda e, ic=ic, tcb=tcb, sl=sl, e_i=e_i: e.activation(
                    silu_r[:, tcb, :], P[ic % 2 * 2 + tcb][:, :], AF.Silu,
                    bias=befm[:, sl, ic:ic + 1]),
                    inc=True, waits=[("sT", lm), ld_be, ("sV", prevc)])
                silu_marks[e_i] = sm
                silu_last[(e_i, ic)] = sm
                if e_i == 0:
                    comb_v[(e_i, ic)] = v_op(
                        lambda e, ic=ic, tcb=tcb, sl=sl: e.tensor_mul(
                            combined[:, ic, tcb * 512:(tcb + 1) * 512],
                            silu_r[:, tcb, :], wbc[:, sl, tcb * 512:(tcb + 1) * 512]),
                        inc=True, waits=[("sS", sm), ("sV", wbc_rdy),
                                         ("sV", prevc)])
                else:
                    v_op(lambda e, ic=ic, tcb=tcb, sl=sl: e.tensor_mul(
                        tmp_bf[:, tcb, :], silu_r[:, tcb, :],
                        wbc[:, sl, tcb * 512:(tcb + 1) * 512]),
                        waits=[("sS", sm), ("sV", wbc_rdy), ("sV", prevc)])
                    comb_v[(e_i, ic)] = v_op(
                        lambda e, ic=ic, tcb=tcb: e.tensor_add(
                            combined[:, ic, tcb * 512:(tcb + 1) * 512],
                            combined[:, ic, tcb * 512:(tcb + 1) * 512],
                            tmp_bf[:, tcb, :]),
                        inc=True)
        wbc_mark[sl] = B.n("sV")
    comb_done_v = B.n("sV")
    PHASE_MARKS.append(("p5", len(B.ops["tensor"])))
    phase_gate("p5")

    ld_pb = load("sLpn", pbfm[:, :], bass.AP(proj_b, 0, [[1, 128], [128, DC]]),
                 slow=True)
    ld_nw = load("sLpn", nwfm[:, :], bass.AP(norm_w, 0, [[1, 128], [128, DC]]),
                 slow=True)
    pbb_v = build_bcast(pb_b, lambda c: pbfm[:, c:c + 1],
                        extra=[("sLpn", 32), ("sT", B.n("sT")),
                               ("sV", comb_done_v)])
    pj_ev = {}
    pslab_free = {}
    experts_t = B.n("sT")
    for fc in range(4):
        for dq in range(8):
            sl = dq % 2
            cw = []
            if fc > 0:
                cw.append(("sT", pslab_free[fc - 1]))
            else:
                cw.append(("sT", experts_t))
            gload(f"sLw{sl}", pslab[:, dq * 2:(dq + 1) * 2, :],
                  bass.AP(proj_w, (dq * 2 * 128) * D + fc * 512,
                          [[D, 128], [128 * D, 2], [1, 512]]), waits=cw)
        pj_all = [("sLw0", 16 * nld["sLw0"]), ("sLw1", 16 * nld["sLw1"])]
        for tc in range(8):
            bank = P[tc % 4]
            lm = 0
            for dc in range(DC):
                w = []
                if dc == 0:
                    w = pj_all + [("sV", comb_done_v)]
                    if tc >= 4:
                        w.append(("sV", pj_ev[(fc, tc - 4)]))
                    elif fc > 0:
                        w.append(("sV", pj_ev[(fc - 1, tc + 4)]))
                lm = mm(bank[:, :], combined[:, dc, tc * 128:(tc + 1) * 128],
                        pslab[:, dc, :], start=(dc == 0), stop=(dc == DC - 1),
                        inc=(dc == DC - 1), waits=w if dc == 0 else ())
            pj_ev[(fc, tc)] = v_op(
                lambda e, bank=bank, tc=tc, fc=fc: e.tensor_add(
                    proj_out[:, tc, fc * 512:(fc + 1) * 512], bank[:, :],
                    pb_b[:, fc * 512:(fc + 1) * 512]),
                inc=True, waits=[("sT", lm), ("sV", pbb_v)])
        pslab_free[fc] = B.n("sT")
    proj_v = B.n("sV")
    proj_t = B.n("sT")
    s3fm = v_op(lambda e: e.tensor_mul(nwfm[:, :], nwfm[:, :],
                                       modfm[:, 5, :]),
                inc=True, waits=[("sLpn", 32)])
    s3b_v = build_bcast(s3_b, lambda c: nwfm[:, c:c + 1],
                        extra=[("sV", s3fm), ("sT", proj_t)])
    PHASE_MARKS.append(("p6a", len(B.ops["tensor"])))
    phase_gate("p6a")

    st_marks = {}
    last_v6 = {}
    h2tm_done = {}
    pj_done = B.n("sT")
    for tc in range(8):
        sl = tc % 2
        hv = []
        for grp in range(4):
            w = [("sV", s3b_v), ("sV", proj_v), ("sT", pj_done)]
            if tc >= 2:
                w.append(last_v6[tc - 2])
            if grp >= 2:
                w.append(("sS", hv[grp - 2]))
            elif tc >= 2:
                w.append(("sS", h2tm_done[tc - 2]))
            lm = 0
            for j in range(4):
                dc = grp * 4 + j
                lm = mm(PS[:, sl * 2 + grp % 2, j * 128:(j + 1) * 128],
                        h2[:, dc, tc * 128:(tc + 1) * 128],
                        ident[:, :], start=True, stop=True,
                        inc=(j == 3), waits=w if j == 0 else ())
            hv.append(s_op(lambda e, grp=grp, sl=sl: e.copy(
                h2tm[:, sl, grp * 512:(grp + 1) * 512],
                PS[:, sl * 2 + grp % 2, :]),
                inc=True, waits=[("sT", lm)]))
        h2tm_done[tc] = hv[-1]
        w_op = v_op if sl == 0 else g_op
        wsem = "sV" if sl == 0 else "sG"
        v1 = w_op(lambda e, tc=tc, sl=sl: e.tensor_add(
            y_buf[:, sl, :], proj_out[:, tc, :], h2tm[:, sl, :]),
            inc=True, waits=[("sS", hv[-1])])
        vm = w_op(lambda e, sl=sl: e.tensor_mul(h2tm[:, sl, :], y_buf[:, sl, :],
                                                y_buf[:, sl, :]),
                  inc=True, waits=[(wsem, v1)])
        v3 = v_op(lambda e, sl=sl: e.reduce_sum(rrms_s[:, sl * 2:sl * 2 + 1],
                                                h2tm[:, sl, :], AX),
                  inc=True, waits=[(wsem, vm)])
        s1 = s_op(lambda e, sl=sl: e.activation(
            rrms_s[:, sl * 2 + 1:sl * 2 + 2], rrms_s[:, sl * 2:sl * 2 + 1],
            AF.Sqrt, bias=consts[:, 1:2], scale=consts[:, 2:3]),
            inc=True, waits=[("sV", v3)])
        v4 = v_op(lambda e, sl=sl: e.reciprocal(
            rrms_s[:, sl * 2 + 1:sl * 2 + 2], rrms_s[:, sl * 2 + 1:sl * 2 + 2]),
            inc=True, waits=[("sS", s1)])
        s2 = s_op(lambda e, sl=sl: e.activation(
            h2tm[:, sl, :], y_buf[:, sl, :], AF.Identity,
            scale=rrms_s[:, sl * 2 + 1:sl * 2 + 2]),
            inc=True, waits=[("sV", v4)])
        v5 = w_op(lambda e, sl=sl: e.tensor_mul(y_buf[:, sl, :], h2tm[:, sl, :],
                                                s3_b[:, :]),
                  waits=[("sS", s2)])
        xw = [(f"sSt{sl}", st_marks.get(tc - 2, 0))]
        v6 = w_op(lambda e, tc=tc, sl=sl: e.tensor_add(
            xout[:, sl, :], y_buf[:, sl, :], x2[:, tc, :]),
            inc=True, waits=xw)
        last_v6[tc] = (wsem, v6)
        B.op("scalar",
             lambda e, tc=tc, sl=sl: e.dma_start(
                 out=bass.AP(yout, tc * 128 * D, [[D, 128], [1, D]]),
                 in_=xout[:, sl, :]),
             inc=(f"sSt{sl}", 16), waits=[(wsem, v6)], auto=False)
        st_marks[tc] = 16 * (tc // 2 + 1)
    B.op("scalar", lambda e: None,
         waits=[("sSt0", 64), ("sSt1", 64)] +
               ([("sDb2", 64)] if DEBUG_DUMPS else []),
         auto=False)

    from contextlib import ExitStack
    with ExitStack() as stack:
        sems = {}
        for nm in LOAD_SEMS + ["sT", "sV", "sS", "sG"]:
            sems[nm] = stack.enter_context(nc.semaphore(nm))
        block = stack.enter_context(nc.Block())

        def run(name):
            def body(eng_obj):
                for fn, inc, waits in B.ops[name]:
                    for s, v in waits:
                        eng_obj.wait_ge(sems[s], v)
                    ins = fn(eng_obj)
                    if inc is not None:
                        s, k = inc
                        assert ins is not None, (name, s)
                        ins.then_inc(sems[s], k)
            return body
        block.sync(run("sync"))
        block.tensor(run("tensor"))
        block.vector(run("vector"))
        block.scalar(run("scalar"))
        block.gpsimd(run("gpsimd"))
    return nc


_NC_CACHE = {}


def _get_nc():
    if "nc" not in _NC_CACHE:
        _NC_CACHE["nc"] = build_program()
    return _NC_CACHE["nc"]


def kernel(L0_attn_hidden_states, L0_attn_wq, L0_attn_wk, L0_attn_wv,
           L0_attn_wo, L0_ffn_router_w, L0_ffn_router_b, L0_ffn_expert_w,
           L0_ffn_expert_b, L0_ffn_proj_w, L0_ffn_proj_b, L0_ffn_norm_w,
           L0_c, L0_wmod):
    x = np.ascontiguousarray(np.asarray(L0_attn_hidden_states, dtype=np.float32))
    wq_t = np.ascontiguousarray(np.asarray(L0_attn_wq, np.float32).T)
    wk_t = np.ascontiguousarray(np.asarray(L0_attn_wk, np.float32).T)
    wv_t = np.ascontiguousarray(np.asarray(L0_attn_wv, np.float32).T)
    wo_t = np.ascontiguousarray(np.asarray(L0_attn_wo, np.float32).T)
    router_w = np.ascontiguousarray(np.asarray(L0_ffn_router_w, np.float32))
    router_b = np.asarray(L0_ffn_router_b, np.float32).reshape(1, E)
    expert_w = np.ascontiguousarray(np.asarray(L0_ffn_expert_w, np.float32))
    expert_b = np.ascontiguousarray(np.asarray(L0_ffn_expert_b, np.float32))
    proj_w = np.ascontiguousarray(np.asarray(L0_ffn_proj_w, np.float32))
    proj_b = np.asarray(L0_ffn_proj_b, np.float32).reshape(1, D)
    norm_w = np.asarray(L0_ffn_norm_w, np.float32).reshape(1, D)
    cmat = np.asarray(L0_c, np.float32)
    wmod = np.ascontiguousarray(np.asarray(L0_wmod, np.float32))

    nc = _get_nc()
    in_maps = []
    for c in range(8):
        b, r0 = c // 4, 1024 * (c % 4)
        xbc = np.ascontiguousarray(np.roll(x[b], -r0, axis=0))
        in_maps.append({
            "xb": xbc, "wq_t": wq_t, "wk_t": wk_t, "wv_t": wv_t, "wo_t": wo_t,
            "router_w": router_w, "router_b": router_b, "expert_w": expert_w,
            "expert_b": expert_b, "proj_w": proj_w, "proj_b": proj_b,
            "norm_w": norm_w, "cvec": cmat[b:b + 1], "wmod": wmod,
        })
    res = run_bass_kernel_spmd(nc, in_maps, core_ids=list(range(8)))
    out = np.empty((2, S, D), np.float32)
    for c in range(8):
        b, r0 = c // 4, 1024 * (c % 4)
        out[b, r0:r0 + 1024] = res.results[c]["yout"]
    return out
